# version 02
# baseline (speedup 1.0000x reference)
import os
import numpy as np
import ml_dtypes
from contextlib import ExitStack

import concourse.bass as bass
import concourse.mybir as mybir
from concourse.bass_utils import run_bass_kernel_spmd

F32 = mybir.dt.float32
BF16 = mybir.dt.bfloat16
I32 = mybir.dt.int32
AF = mybir.ActivationFunctionType
ALU = mybir.AluOpType

NC = 8
TLOC = 1024
D = 4096
PT = 128
NDT = D // PT
NTT = TLOC // PT
EPS = 1e-5

BF = ml_dtypes.bfloat16


class Ctx:
    def __init__(self, nc):
        self.nc = nc
        self.q = {"sync": [], "gpsimd": [], "tensor": [], "vector": [], "scalar": []}
        self.sems = {}
        self.stack = ExitStack()
        self.uid = 0

    def name(self, p):
        self.uid += 1
        return f"{p}{self.uid}"

    def sem(self, name):
        if name not in self.sems:
            h = self.stack.enter_context(self.nc.semaphore(name))
            self.sems[name] = [h, 0]
        return self.sems[name][0]

    def val(self, name):
        if name not in self.sems:
            self.sem(name)
        return self.sems[name][1]

    def emit(self, q, fn, sem=None, inc=0):
        h = None
        post = None
        if sem is not None:
            h = self.sem(sem)
            self.sems[sem][1] += inc
            post = self.sems[sem][1]

        def run(eng, fn=fn, h=h, inc=inc):
            ins = fn(eng)
            if h is not None:
                ins.then_inc(h, inc)

        self.q[q].append(run)
        return post

    def emit_cc(self, fn, sem):
        h = self.sem(sem)
        self.sems[sem][1] += 1

        def run(eng, fn=fn, h=h):
            fn(eng).then_inc(h)

        self.q["gpsimd"].append(run)
        return self.sems[sem][1]

    def barrier(self, name):
        h = self.sem(name)
        self.sems[name][1] += 5
        tgt = self.sems[name][1]
        for q in ("sync", "gpsimd", "tensor", "vector", "scalar"):
            self.q[q].append(lambda eng, h=h: eng.drain().then_inc(h, 1))
        for q in ("sync", "gpsimd", "tensor", "vector", "scalar"):
            self.q[q].append(lambda eng, h=h, tgt=tgt: eng.wait_ge(h, tgt))

    def wait(self, q, sem, val):
        h = self.sem(sem)
        self.q[q].append(lambda eng, h=h, val=val: eng.wait_ge(h, val))

    def ccol(self, val):
        if not hasattr(self, "_ccols"):
            self._ccols = {}
        if val not in self._ccols:
            t = self.sbuf(self.name("cc"), [PT, 1], F32)
            self.emit("gpsimd", lambda e, t=t, val=val: e.memset(t[:, :], val),
                      sem="gcm", inc=1)
            self._ccols[val] = t
        return self._ccols[val][:, 0:1]

    def sbuf(self, name, shape, dtype):
        if not hasattr(self, "_sbg"):
            self._sbg = {}
        g = self.nc.sbuf_tensor(name, shape, dtype)
        t = g.__enter__()
        self._sbg[name] = g
        return t

    def free_sbuf(self, *names):
        for n in names:
            self._sbg.pop(n).__exit__(None, None, None)

    def psum(self, name, shape, dtype=F32):
        if not hasattr(self, "_psg"):
            self._psg = {}
        g = self.nc.psum_tensor(name, shape, dtype)
        t = g.__enter__()
        self._psg[name] = g
        return t

    def free_psum(self, *names):
        for n in names:
            self._psg.pop(n).__exit__(None, None, None)

    def realize(self):
        nc = self.nc
        with self.stack:
            with nc.Block() as blk:
                @blk.sync
                def _(e):
                    for f in self.q["sync"]:
                        f(e)

                @blk.gpsimd
                def _(e):
                    for f in self.q["gpsimd"]:
                        f(e)

                @blk.tensor
                def _(e):
                    for f in self.q["tensor"]:
                        f(e)

                @blk.vector
                def _(e):
                    for f in self.q["vector"]:
                        f(e)

                @blk.scalar
                def _(e):
                    for f in self.q["scalar"]:
                        f(e)
        return nc


RACE_DETECT = False


def mk_nc():
    return bass.Bass("TRN2", target_bir_lowering=False,
                     detect_race_conditions=RACE_DETECT)


def emit_identity(c, ident):
    c.emit("gpsimd", lambda e: e.memset(ident[:, :], 1.0), sem="gcm", inc=1)
    c.wait("gpsimd", "gcm", c.val("gcm"))
    c.emit("gpsimd", lambda e: e.affine_select(
        out=ident[:, :], in_=ident[:, :], compare_op=ALU.is_equal, fill=0.0,
        base=0, pattern=[[-1, PT]], channel_multiplier=1), sem="gcm", inc=1)


def emit_ln_tile(c, xtile, xn_out, scratch, stats, extra_mul=None):
    W = xtile.shape[1]
    sx = stats[:, 0:1]
    negmean = stats[:, 1:2]
    ssq = stats[:, 2:3]
    rstd = stats[:, 3:4]
    c.emit("vector", lambda e: e.reduce_sum(sx, xtile[:, :], axis=mybir.AxisListType.X),
           sem="lnv", inc=1)
    c.wait("vector", "lnv", c.val("lnv"))
    c.emit("vector", lambda e: e.tensor_scalar(
        out=negmean, in0=sx, scalar1=-1.0 / W, scalar2=None, op0=ALU.mult),
        sem="lnv", inc=1)
    c.wait("scalar", "lnv", c.val("lnv"))
    eps_col = c.ccol(EPS)
    c.wait("scalar", "gcm", c.val("gcm"))
    c.emit("scalar", lambda e: e.activation(
        scratch[:, :], xtile[:, :], AF.Square, bias=negmean, scale=1.0,
        accum_out=ssq), sem="lns", inc=1)
    c.wait("scalar", "lns", c.val("lns"))
    c.emit("scalar", lambda e: e.activation(
        rstd, ssq, AF.Ln, bias=eps_col, scale=1.0 / W), sem="lns", inc=1)
    c.wait("scalar", "lns", c.val("lns"))
    c.emit("scalar", lambda e: e.activation(
        rstd, rstd, AF.Exp, scale=-0.5), sem="lns", inc=1)
    c.wait("vector", "lns", c.val("lns"))
    c.emit("vector", lambda e: e.tensor_scalar(
        out=xn_out[:, :], in0=xtile[:, :], scalar1=negmean, scalar2=rstd,
        op0=ALU.add, op1=ALU.mult), sem="lnv", inc=1)
    if extra_mul is not None:
        c.emit("vector", lambda e: e.tensor_scalar(
            out=xn_out[:, :], in0=xn_out[:, :], scalar1=extra_mul, scalar2=None,
            op0=ALU.mult), sem="lnv", inc=1)
    return c.val("lnv")


def emit_phase0(c, x0_ap, xnT_dram_ap):
    nc = c.nc
    ident = c.sbuf("p0_ident", [PT, PT], BF16)
    emit_identity(c, ident)
    xtile = [c.sbuf(f"p0_x{i}", [PT, D], F32) for i in range(2)]
    xnbf = [c.sbuf(f"p0_xn{i}", [PT, D], BF16) for i in range(2)]
    scratch = c.sbuf("p0_scr", [PT, D], F32)
    stats = c.sbuf("p0_st", [PT, 4], F32)
    xnT_loc = c.sbuf("p0_xnT", [PT, NDT * TLOC], BF16)
    ptr = [c.psum(f"p0_ptr{i}", [PT, 512], BF16) for i in range(2)]

    c.wait("tensor", "gcm", c.val("gcm"))
    for i in range(NTT):
        b = i % 2
        if i >= 2:
            c.wait("sync", "p0_tp", 32 * (i - 1))
            c.wait("vector", "p0_tp", 32 * (i - 1))
        c.emit("sync", lambda e, i=i, b=b: e.dma_start(
            out=xtile[b][:, :], in_=x0_ap[i * PT:(i + 1) * PT, :]), sem="p0_ld", inc=16)
        c.wait("vector", "p0_ld", 16 * (i + 1))
        emit_ln_tile(c, xtile[b], xnbf[b], scratch, stats)
        lnv = c.val("lnv")
        c.wait("tensor", "lnv", lnv)
        for j in range(NDT):
            pb = (j // 4) % 2
            sl = (j % 4) * PT
            if j % 4 == 0 and (i * NDT + j) >= 8:
                c.wait("tensor", "p0_evac", i * 8 + j // 4 - 1)
            c.emit("tensor", lambda e, b=b, j=j, pb=pb, sl=sl: e.transpose(
                ptr[pb][:, sl:sl + PT], xnbf[b][:, j * PT:(j + 1) * PT],
                ident[:, :]), sem="p0_tp", inc=1)
            if j % 4 == 3:
                c.wait("vector", "p0_tp", c.val("p0_tp"))
                j0 = j - 3
                c.emit("vector", lambda e, b=b, pb=pb, i=i, j0=j0: e.tensor_copy(
                    xnT_loc[:, :].rearrange("p (j t) -> p j t", t=TLOC)
                    [:, j0:j0 + 4, i * PT:(i + 1) * PT],
                    ptr[pb][:, :].rearrange("p (j t) -> p j t", t=PT)),
                    sem="p0_evac", inc=1)
    c.wait("sync", "p0_evac", c.val("p0_evac"))
    c.emit("sync", lambda e: e.dma_start(
        out=xnT_dram_ap.rearrange("(j p) t -> p j t", p=PT),
        in_=xnT_loc[:, :].rearrange("p (j t) -> p j t", t=TLOC)), sem="p0_st", inc=16)
    c.wait("sync", "p0_st", c.val("p0_st"))
    return xnT_loc


P1_SEG = [128] * 10 + [8]
NT512 = 16


def emit_phase1(c, io, ntiles=NT512):
    nc = c.nc
    ident = c.sbuf("p1_id", [PT, PT], BF16)
    emit_identity(c, ident)
    identf = c.sbuf("p1_idf", [PT, PT], F32)
    c.emit("gpsimd", lambda e: e.memset(identf[:, :], 1.0), sem="gcm", inc=1)
    c.wait("gpsimd", "gcm", c.val("gcm"))
    c.emit("gpsimd", lambda e: e.affine_select(
        out=identf[:, :], in_=identf[:, :], compare_op=ALU.is_equal, fill=0.0,
        base=0, pattern=[[-1, PT]], channel_multiplier=1), sem="gcm", inc=1)
    maskNEG = c.sbuf("p1_maskn", [PT, PT], F32)
    c.emit("gpsimd", lambda e: e.memset(maskNEG[:, :], 0.0), sem="gcm", inc=1)
    c.wait("gpsimd", "gcm", c.val("gcm"))
    c.emit("gpsimd", lambda e: e.affine_select(
        out=maskNEG[:, :], in_=maskNEG[:, :], compare_op=ALU.is_ge, fill=-1e30,
        base=0, pattern=[[1, PT]], channel_multiplier=-1), sem="gcm", inc=1)
    SEL8 = c.sbuf("p1_sel8", [8, 512], F32)
    c.emit("gpsimd", lambda e: e.memset(SEL8[:, :], 1.0), sem="gcm", inc=1)
    c.wait("gpsimd", "gcm", c.val("gcm"))
    c.emit("gpsimd", lambda e: e.affine_select(
        out=SEL8[:, :], in_=SEL8[:, :], compare_op=ALU.is_equal, fill=0.0,
        base=0, pattern=[[-1, 8], [0, 64]], channel_multiplier=1), sem="gcm", inc=1)
    SEL8b = c.sbuf("p1_sel8b", [8, 1024], F32)
    c.emit("gpsimd", lambda e: e.memset(SEL8b[:, :], 1.0), sem="gcm", inc=1)
    c.wait("gpsimd", "gcm", c.val("gcm"))
    c.emit("gpsimd", lambda e: e.affine_select(
        out=SEL8b[:, :], in_=SEL8b[:, :], compare_op=ALU.is_equal, fill=0.0,
        base=0, pattern=[[-1, 8], [0, 128]], channel_multiplier=1), sem="gcm", inc=1)
    ones1 = c.sbuf("p1_ones1", [1, PT], F32)
    c.emit("gpsimd", lambda e: e.memset(ones1[:, :], 1.0), sem="gcm", inc=1)
    ones8 = c.sbuf("p1_ones8", [8, PT], F32)
    c.emit("gpsimd", lambda e: e.memset(ones8[:, :], 1.0), sem="gcm", inc=1)

    A8 = c.sbuf("p1_A8", [8, 1], F32)
    dtb8 = c.sbuf("p1_dtb", [8, 1], F32)
    D8r = c.sbuf("p1_D8r", [1, 8], F32)
    Drow = c.sbuf("p1_Drow", [1, 512], F32)
    Dexp = c.sbuf("p1_Dexp", [PT, 512], F32)
    nw4 = c.sbuf("p1_nw4", [PT, 4], F32)
    cwT = c.sbuf("p1_cw", [PT, 24], F32)
    cbT = c.sbuf("p1_cb", [PT, 6], F32)
    c.emit("sync", lambda e: e.dma_start(
        out=A8[:, :], in_=io["A_log"].unsqueeze(1)), sem="p1_pl", inc=16)
    c.emit("sync", lambda e: e.dma_start(
        out=dtb8[:, :], in_=io["dt_bias"].unsqueeze(1)), sem="p1_pl", inc=16)
    c.emit("sync", lambda e: e.dma_start(
        out=D8r[:, :], in_=io["Dv"].unsqueeze(0)), sem="p1_pl", inc=16)
    c.emit("sync", lambda e: e.dma_start(
        out=nw4[:, :], in_=io["norm_w"].rearrange("(j p) -> p j", p=PT),
        allow_slow_non_contiguous=True), sem="p1_pl", inc=16)
    c.emit("sync", lambda e: e.dma_start(
        out=cwT[:, :].rearrange("p (j k) -> p j k", k=4),
        in_=io["conv_w"].rearrange("(j p) k -> p j k", p=PT)), sem="p1_pl", inc=16)
    c.emit("sync", lambda e: e.dma_start(
        out=cbT[:, :], in_=io["conv_b"].rearrange("(j p) -> p j", p=PT),
        allow_slow_non_contiguous=True), sem="p1_pl", inc=16)
    c.wait("scalar", "p1_pl", 16)
    c.emit("scalar", lambda e: e.activation(A8[:, :], A8[:, :], AF.Exp, scale=1.0),
           sem="p1_sm", inc=1)
    c.wait("vector", "p1_sm", c.val("p1_sm"))
    c.emit("vector", lambda e: e.tensor_scalar(
        out=A8[:, :], in0=A8[:, :], scalar1=-1.0, scalar2=None, op0=ALU.mult),
        sem="p1_vm", inc=1)
    c.wait("vector", "p1_pl", 48)
    c.wait("vector", "gcm", c.val("gcm"))
    for h in range(8):
        c.emit("vector", lambda e, h=h: e.tensor_scalar(
            out=Drow[0:1, 64 * h:64 * h + 64], in0=ones1[0:1, 0:64],
            scalar1=D8r[0:1, h:h + 1], scalar2=None, op0=ALU.mult),
            sem="p1_vm", inc=1)
    Aps0 = c.psum("p1_Aps", [PT, 512], F32)
    c.wait("tensor", "p1_vm", c.val("p1_vm"))
    c.emit("tensor", lambda e: e.matmul(
        Aps0[:, :], ones1[0:1, :], Drow[0:1, :], start=True, stop=True),
        sem="p1_tm", inc=1)
    c.wait("vector", "p1_tm", c.val("p1_tm"))
    c.emit("vector", lambda e: e.tensor_copy(Dexp[:, :], Aps0[:, :]),
           sem="p1_vm", inc=1)

    WinT = c.sbuf("p1_win", [PT, NDT * 1288], BF16)
    c.emit("gpsimd", lambda e: e.dma_start(
        out=WinT[:, :].rearrange("p (j q) -> p j q", q=1288),
        in_=io["WinT"].rearrange("(j p) q -> p j q", p=PT)), sem="p1_wl", inc=16)

    xr = c.sbuf("p1_xr", [PT, NDT * 512], BF16)
    sgT = [c.sbuf(f"p1_sg{j}", [PT, 512], BF16) for j in range(4)]
    cin = [c.sbuf(f"p1_cin{j}", [PT, 515], BF16) for j in range(6)]
    cacc = c.sbuf("p1_cacc", [PT, 512], F32)
    xconv = [c.sbuf(f"p1_xc{j}", [PT, 512], BF16) for j in range(6)]
    dtT = c.sbuf("p1_dtT", [8, 512], F32)
    dtp = c.sbuf("p1_dtp", [8, 512], F32)
    dlog = c.sbuf("p1_dlog", [8, 512], F32)
    spa = c.sbuf("p1_spa", [8, 512], F32)
    spb = c.sbuf("p1_spb", [8, 512], F32)
    aT = c.sbuf("p1_aT", [8, PT], F32)
    ea8 = c.sbuf("p1_ea8", [8, PT], F32)
    w8 = c.sbuf("p1_w8", [8, PT], F32)
    ee8 = c.sbuf("p1_ee8", [8, PT], F32)
    selA = c.sbuf("p1_selA", [8, 1024], F32)
    aTT = c.sbuf("p1_aTT", [PT, 8], F32)
    scrD = c.sbuf("p1_scrD", [PT, 512], F32)
    Ldec8 = c.sbuf("p1_ldec", [PT, 1024], BF16)
    xs_chunk = c.sbuf("p1_xsc", [PT, 512], BF16)
    xtil = c.sbuf("p1_xtil", [PT, 512], BF16)
    xhat = c.sbuf("p1_xhat", [PT, 512], BF16)
    Bt_chunk = c.sbuf("p1_btc", [PT, PT], BF16)
    Mm = c.sbuf("p1_mm", [PT, PT], BF16)
    scr3 = c.sbuf("p1_scr3", [PT, 512], F32)
    EA_sb = c.sbuf("p1_ea", [PT, 512], F32)
    S_f = c.sbuf("p1_S", [PT, 512], F32)
    S_bf = c.sbuf("p1_Sbf", [PT, 512], BF16)
    ysb = c.sbuf("p1_ysb", [PT, 512], F32)
    ynp = c.sbuf("p1_ynp", [PT, 512], BF16)
    ynT_sb = [c.sbuf(f"p1_ynT{j}", [PT, 512], BF16) for j in range(4)]
    scr2 = c.sbuf("p1_scr2", [PT, 512], F32)
    gst = c.sbuf("p1_gst", [PT, 4], F32)
    eps_col = c.ccol(EPS)
    one_col = c.ccol(1.0)

    mmps = [c.psum(f"p1_mmps{i}", [PT, 512], F32) for i in range(2)]
    Aps = Aps0
    Yps = c.psum("p1_Yps", [PT, 512], F32)
    Y2ps = c.psum("p1_Y2ps", [PT, 512], F32)
    Tps = c.psum("p1_Tps", [PT, 512], F32)
    TRb = c.psum("p1_TRb", [PT, 512], BF16)
    AUps = c.psum("p1_AUps", [PT, 512], F32)

    c.emit("gpsimd", lambda e: e.memset(S_f[:, :], 0.0), sem="gcm", inc=1)
    c.emit("gpsimd", lambda e: e.memset(S_bf[:, :], 0.0), sem="gcm", inc=1)
    for j in range(6):
        c.emit("gpsimd", lambda e, j=j: e.memset(cin[j][:, 0:3], 0.0),
               sem="gcm", inc=1)
    gcm_init = c.val("gcm")

    c.wait("tensor", "gcm", gcm_init)
    c.wait("tensor", "p1_wl", 16)
    c.wait("vector", "gcm", gcm_init)

    n_sd = 0
    for i in range(ntiles):
        r, tl = i // 2, (i % 2) * 512
        if i > 0:
            c.wait("sync", "p1_tm", c.tm_tile_end)
        c.emit("sync", lambda e, r=r, tl=tl: e.dma_start(
            out=xr[:, :].rearrange("p (j t) -> p j t", t=512),
            in_=io["xnT_full"][4096 * r:4096 * (r + 1), tl:tl + 512]
            .rearrange("(j p) t -> p j t", p=PT)), sem="p1_rl", inc=16)
        c.wait("tensor", "p1_rl", 16 * (i + 1))

        if i > 0:
            c.wait("vector", "p1_vm", c.conv_done)
            for j in range(6):
                c.emit("vector", lambda e, j=j: e.tensor_copy(
                    cin[j][:, 0:3], cin[j][:, 512:515]), sem="p1_vm", inc=1)
        halo_done = c.val("p1_vm")

        WinTv = WinT[:, :].rearrange("p (j q) -> p j q", q=1288)
        xrv = xr[:, :].rearrange("p (j t) -> p j t", t=512)
        if i > 0:
            c.wait("tensor", "p1_ev", c.ev_count)
        for pc in range(11):
            m = P1_SEG[pc]
            q0 = 128 * pc
            pb = pc % 2
            if pc >= 2:
                c.wait("tensor", "p1_ev", c.ev_count - 1)
            for j in range(NDT):
                c.emit("tensor", lambda e, j=j, q0=q0, m=m, pb=pb: e.matmul(
                    mmps[pb][0:m, :], WinTv[:, j, q0:q0 + m], xrv[:, j, :],
                    start=(j == 0), stop=(j == NDT - 1)),
                    sem="p1_tm", inc=(1 if j == NDT - 1 else 0))
            mm_done = c.val("p1_tm")
            if pc < 4:
                c.wait("scalar", "p1_tm", mm_done)
                c.emit("scalar", lambda e, pc=pc, pb=pb: e.activation(
                    sgT[pc][:, :], mmps[pb][:, :], AF.Silu, scale=1.0),
                    sem="p1_ev", inc=1)
            elif pc < 10:
                j6 = pc - 4
                c.wait("vector", "p1_tm", mm_done)
                if i > 0 or pc > 4:
                    pass
                c.wait("vector", "p1_vm", halo_done)
                c.emit("vector", lambda e, j6=j6, pb=pb: e.tensor_copy(
                    cin[j6][:, 3:515], mmps[pb][:, :]), sem="p1_ev", inc=1)
            else:
                c.wait("vector", "p1_tm", mm_done)
                if i > 0:
                    c.wait("vector", "p1_sm", c.dt_sm_end)
                c.emit("vector", lambda e, pb=pb: e.tensor_copy(
                    dtT[:, :], mmps[pb][0:8, :]), sem="p1_ev", inc=1)
            c.ev_count = c.val("p1_ev")
        c.tm_tile_end = c.val("p1_tm")
        evac_all = c.val("p1_ev")

        c.wait("vector", "p1_ev", evac_all)
        if i > 0:
            c.wait("scalar", "p1_tm", c.sdd_tm_end)
            c.wait("scalar", "p1_vm", c.trb_free)
        for j in range(6):
            c.emit("vector", lambda e, j=j: e.tensor_scalar(
                out=cacc[:, :], in0=cin[j][:, 0:512], scalar1=cwT[:, 4 * j:4 * j + 1],
                scalar2=cbT[:, j:j + 1], op0=ALU.mult, op1=ALU.add),
                sem="p1_vm", inc=0)
            for k in range(1, 4):
                c.emit("vector", lambda e, j=j, k=k: e.scalar_tensor_tensor(
                    out=cacc[:, :], in0=cin[j][:, k:k + 512],
                    scalar=cwT[:, 4 * j + k:4 * j + k + 1], in1=cacc[:, :],
                    op0=ALU.mult, op1=ALU.add), sem="p1_vm", inc=(1 if k == 3 else 0))
            c.wait("scalar", "p1_vm", c.val("p1_vm"))
            c.emit("scalar", lambda e, j=j: e.activation(
                xconv[j][:, :], cacc[:, :], AF.Silu, scale=1.0), sem="p1_sl", inc=1)
            c.wait("vector", "p1_sl", c.val("p1_sl"))
        c.conv_done = c.val("p1_vm")
        c.sl_end = c.val("p1_sl")

        c.wait("scalar", "p1_ev", evac_all)
        c.emit("scalar", lambda e: e.activation(
            spa[:, :], dtT[:, :], AF.Abs, bias=dtb8[:, 0:1], scale=1.0),
            sem="p1_sm", inc=1)
        c.wait("scalar", "p1_sm", c.val("p1_sm"))
        c.emit("scalar", lambda e: e.activation(
            spa[:, :], spa[:, :], AF.Exp, scale=-1.0), sem="p1_sm", inc=1)
        c.wait("scalar", "p1_sm", c.val("p1_sm"))
        c.emit("scalar", lambda e: e.activation(
            spa[:, :], spa[:, :], AF.Ln, bias=one_col[0:8, 0:1], scale=1.0),
            sem="p1_sm", inc=1)
        c.wait("scalar", "p1_sm", c.val("p1_sm"))
        c.emit("scalar", lambda e: e.activation(
            spb[:, :], dtT[:, :], AF.Relu, bias=dtb8[:, 0:1], scale=1.0),
            sem="p1_sm", inc=1)
        c.dt_sm_end = c.val("p1_sm")
        c.wait("vector", "p1_sm", c.val("p1_sm"))
        c.emit("vector", lambda e: e.tensor_add(dtp[:, :], spa[:, :], spb[:, :]),
               sem="p1_vm", inc=1)
        c.wait("vector", "p1_vm", c.val("p1_vm"))
        c.emit("vector", lambda e: e.tensor_scalar(
            out=dlog[:, :], in0=dtp[:, :], scalar1=A8[:, 0:1], scalar2=None,
            op0=ALU.mult), sem="p1_vm", inc=1)
        c.wait("vector", "p1_vm", c.val("p1_vm"))

        for q in range(4):
            ql = 128 * q
            if i > 0 or q > 0:
                c.wait("vector", "p1_sm", c.sm_eam_prev)
                c.wait("vector", "p1_tm", c.eexp_prev)
                c.wait("scalar", "p1_tm", c.eexp_prev)
            c.emit("vector", lambda e, ql=ql: e.tensor_tensor_scan(
                aT[:, :], dlog[:, ql:ql + PT], dlog[:, ql:ql + PT], 0.0,
                ALU.add, ALU.bypass), sem="p1_vm", inc=1)
            scan_done = c.val("p1_vm")
            c.wait("scalar", "p1_vm", scan_done)
            c.emit("scalar", lambda e: e.activation(
                ea8[:, :], aT[:, :], AF.Exp, scale=1.0), sem="p1_sm", inc=1)
            c.emit("scalar", lambda e: e.activation(
                w8[:, :], aT[:, :], AF.Exp, scale=-1.0, bias=aT[:, 127:128]),
                sem="p1_sm", inc=1)
            sm_eam = c.val("p1_sm")
            c.sm_eam_prev = sm_eam
            c.wait("vector", "p1_sm", sm_eam)
            c.emit("vector", lambda e, ql=ql: e.tensor_tensor(
                out=w8[:, :], in0=w8[:, :], in1=dtp[:, ql:ql + PT],
                op=ALU.mult), sem="p1_vm", inc=1)
            c.wait("vector", "p1_vm", c.val("p1_vm"))
            c.emit("vector", lambda e: e.tensor_scalar(
                out=ee8[:, :], in0=ones8[:, :],
                scalar1=ea8[:, 127:128], scalar2=None, op0=ALU.mult),
                sem="p1_vm", inc=1)
            c.emit("vector", lambda e: e.tensor_tensor(
                out=selA[:, :].rearrange("p (h u) -> p h u", u=PT),
                in0=aT[:, :].unsqueeze(1).broadcast_to([8, 8, PT]),
                in1=SEL8b[:, :].rearrange("p (h u) -> p h u", u=PT),
                op=ALU.mult), sem="p1_vm", inc=1)
            eam_done = c.val("p1_vm")

            if i > 0 or q > 0:
                c.wait("tensor", "p1_vm", c.yevac_prev)
            c.wait("tensor", "p1_vm", eam_done)
            c.emit("tensor", lambda e: e.transpose(
                Y2ps[:, 0:8], aT[:, :], identf[0:8, 0:8]), sem="p1_tm", inc=1)
            c.att_prev = c.val("p1_tm")
            c.wait("vector", "p1_tm", c.att_prev)
            c.emit("vector", lambda e: e.tensor_copy(aTT[:, :], Y2ps[:, 0:8]),
                   sem="p1_vm", inc=1)
            att_evac = c.val("p1_vm")

            c.wait("tensor", "p1_sl", c.sl_end)
            if i > 0 or q > 0:
                c.wait("tensor", "p1_vm", c.trb_free)
            for j2 in range(4):
                c.emit("tensor", lambda e, j2=j2, ql=ql: e.transpose(
                    TRb[:, 128 * j2:128 * j2 + 128],
                    xconv[j2][:, ql:ql + PT], ident[:, :]), sem="p1_tm", inc=1)
            c.wait("vector", "p1_tm", c.val("p1_tm"))
            c.emit("vector", lambda e: e.tensor_copy(xs_chunk[:, :], TRb[:, :]),
                   sem="p1_vm", inc=1)
            c.wait("tensor", "p1_vm", c.val("p1_vm"))
            c.emit("tensor", lambda e, ql=ql: e.transpose(
                TRb[:, 0:128], xconv[4][:, ql:ql + PT], ident[:, :]),
                sem="p1_tm", inc=1)
            c.wait("vector", "p1_tm", c.val("p1_tm"))
            c.emit("vector", lambda e: e.tensor_copy(Bt_chunk[:, :], TRb[:, 0:128]),
                   sem="p1_vm", inc=1)

            if i > 0 or q > 0:
                c.wait("tensor", "p1_vm", c.supd_prev)
            c.emit("tensor", lambda e, ql=ql: e.matmul(
                Aps[:, 0:128], xconv[4][:, ql:ql + PT], xconv[5][:, ql:ql + PT],
                start=True, stop=True), sem="p1_tm", inc=1)
            gram_done = c.val("p1_tm")
            c.wait("vector", "p1_tm", gram_done)
            c.emit("vector", lambda e: e.tensor_copy(Mm[:, :], Aps[:, 0:128]),
                   sem="p1_vm", inc=1)
            mm_evac = c.val("p1_vm")
            c.wait("tensor", "p1_vm", mm_evac)
            c.emit("tensor", lambda e, ql=ql: e.matmul(
                Aps[:, :], dtp[:, ql:ql + PT], SEL8[:, :], start=True, stop=True),
                sem="p1_tm", inc=1)
            c.wait("vector", "p1_tm", c.val("p1_tm"))
            c.emit("vector", lambda e: e.tensor_tensor(
                out=xtil[:, :], in0=xs_chunk[:, :], in1=Aps[:, :], op=ALU.mult),
                sem="p1_vm", inc=1)
            xtil_done = c.val("p1_vm")

            for g in range(2):
                c.wait("tensor", "p1_vm",
                       xtil_done if g == 0 else c.val("p1_vm"))
                c.emit("tensor", lambda e, g=g: e.matmul(
                    AUps[:, :], ones8[:, :], selA[:, 512 * g:512 * g + 512],
                    start=True, stop=True), sem="p1_tm", inc=1)
                au_done = c.val("p1_tm")
                c.wait("vector", "p1_tm", au_done)
                c.wait("vector", "p1_vm", att_evac)
                for hh in range(4):
                    h = 4 * g + hh
                    c.emit("vector", lambda e, hh=hh, h=h: e.scalar_tensor_tensor(
                        out=scrD[:, 128 * hh:128 * hh + 128],
                        in0=AUps[:, 128 * hh:128 * hh + 128],
                        scalar=aTT[:, h:h + 1], in1=maskNEG[:, :],
                        op0=ALU.subtract, op1=ALU.add), sem="p1_vm", inc=1)
                diff_done = c.val("p1_vm")
                if g == 1:
                    pass
                c.wait("scalar", "p1_vm", diff_done)
                c.emit("scalar", lambda e: e.activation(
                    scrD[:, :], scrD[:, :], AF.Exp, scale=1.0), sem="p1_sm", inc=1)
                exp_done = c.val("p1_sm")
                c.wait("vector", "p1_sm", exp_done)
                for hh in range(4):
                    h = 4 * g + hh
                    c.emit("vector", lambda e, hh=hh, h=h: e.tensor_tensor(
                        out=Ldec8[:, 128 * h:128 * h + 128],
                        in0=scrD[:, 128 * hh:128 * hh + 128], in1=Mm[:, :],
                        op=ALU.mult), sem="p1_vm", inc=1)
            ldec_done = c.val("p1_vm")

            c.wait("tensor", "p1_vm", ldec_done)
            for h in range(8):
                hc = 64 * h
                c.emit("tensor", lambda e, h=h, hc=hc: e.matmul(
                    Yps[:, hc:hc + 64], Ldec8[:, 128 * h:128 * h + 128],
                    xtil[:, hc:hc + 64], start=True, stop=True),
                    sem="p1_tm", inc=0)
                c.emit("tensor", lambda e, hc=hc, ql=ql: e.matmul(
                    Y2ps[:, hc:hc + 64], xconv[5][:, ql:ql + PT],
                    S_bf[:, hc:hc + 64], start=True, stop=True),
                    sem="p1_tm", inc=(1 if h == 7 else 0))
            ymm_done = c.val("p1_tm")

            c.emit("tensor", lambda e: e.matmul(
                Aps[:, :], w8[:, :], SEL8[:, :], start=True, stop=True),
                sem="p1_tm", inc=1)
            c.wait("vector", "p1_tm", c.val("p1_tm"))
            c.emit("vector", lambda e: e.tensor_tensor(
                out=xhat[:, :], in0=xs_chunk[:, :], in1=Aps[:, :], op=ALU.mult),
                sem="p1_vm", inc=1)
            xhat_done = c.val("p1_vm")
            c.wait("tensor", "p1_vm", xhat_done)
            for h in range(8):
                hc = 64 * h
                c.emit("tensor", lambda e, hc=hc: e.matmul(
                    Tps[:, hc:hc + 64], Bt_chunk[:, :], xhat[:, hc:hc + 64],
                    start=True, stop=True), sem="p1_tm", inc=(1 if h == 7 else 0))
            tmm_done = c.val("p1_tm")

            c.emit("tensor", lambda e: e.matmul(
                Aps[:, :], ea8[:, :], SEL8[:, :], start=True, stop=True),
                sem="p1_tm", inc=1)
            ea_done = c.val("p1_tm")
            c.wait("vector", "p1_tm", ea_done)
            c.emit("vector", lambda e: e.tensor_copy(EA_sb[:, :], Aps[:, :]),
                   sem="p1_vm", inc=1)
            easb_done = c.val("p1_vm")
            c.emit("vector", lambda e: e.scalar_tensor_tensor(
                out=ysb[:, :], in0=Y2ps[:, :], scalar=1.0, in1=EA_sb[:, :],
                op0=ALU.bypass, op1=ALU.mult), sem="p1_vm", inc=1)
            c.wait("vector", "p1_vm", c.val("p1_vm"))
            c.emit("vector", lambda e: e.tensor_add(ysb[:, :], ysb[:, :], Yps[:, :]),
                   sem="p1_vm", inc=1)
            c.emit("vector", lambda e: e.tensor_tensor(
                out=scr2[:, :], in0=xs_chunk[:, :], in1=Dexp[:, :], op=ALU.mult),
                sem="p1_vm", inc=1)
            c.wait("vector", "p1_vm", c.val("p1_vm"))
            c.emit("vector", lambda e: e.tensor_add(ysb[:, :], ysb[:, :], scr2[:, :]),
                   sem="p1_vm", inc=1)
            yevac_done = c.val("p1_vm")
            c.yevac_prev = yevac_done

            c.wait("tensor", "p1_vm", easb_done)
            c.emit("tensor", lambda e: e.matmul(
                Aps[:, :], ee8[:, :], SEL8[:, :], start=True, stop=True),
                sem="p1_tm", inc=1)
            eexp_done = c.val("p1_tm")
            c.eexp_prev = eexp_done
            c.wait("vector", "p1_tm", eexp_done)
            c.emit("vector", lambda e: e.tensor_tensor(
                out=S_f[:, :], in0=S_f[:, :], in1=Tps[:, :], op=ALU.add),
                sem="p1_vm", inc=1)
            c.wait("vector", "p1_vm", c.val("p1_vm"))
            c.emit("vector", lambda e: e.tensor_tensor(
                out=S_f[:, :], in0=S_f[:, :], in1=Aps[:, :], op=ALU.mult),
                sem="p1_vm", inc=1)
            c.wait("vector", "p1_vm", c.val("p1_vm"))
            c.emit("vector", lambda e: e.tensor_copy(S_bf[:, :], S_f[:, :]),
                   sem="p1_vm", inc=1)
            c.supd_prev = c.val("p1_vm")

            c.wait("scalar", "p1_vm", yevac_done)
            c.emit("scalar", lambda e: e.activation(
                scr3[:, :], ysb[:, :], AF.Square, scale=1.0,
                accum_out=gst[:, 0:1]), sem="p1_sm", inc=1)
            c.wait("scalar", "p1_sm", c.val("p1_sm"))
            c.emit("scalar", lambda e: e.activation(
                gst[:, 1:2], gst[:, 0:1], AF.Ln, bias=eps_col, scale=1.0 / 512),
                sem="p1_sm", inc=1)
            c.wait("scalar", "p1_sm", c.val("p1_sm"))
            c.emit("scalar", lambda e: e.activation(
                gst[:, 1:2], gst[:, 1:2], AF.Exp, scale=-0.5), sem="p1_sm", inc=1)
            c.wait("vector", "p1_sm", c.val("p1_sm"))
            c.emit("vector", lambda e: e.tensor_scalar(
                out=ynp[:, :], in0=ysb[:, :], scalar1=gst[:, 1:2], scalar2=None,
                op0=ALU.mult), sem="p1_vm", inc=1)
            ynp_done = c.val("p1_vm")
            c.wait("tensor", "p1_vm", ynp_done)
            for j2 in range(4):
                c.emit("tensor", lambda e, j2=j2: e.transpose(
                    TRb[:, 128 * j2:128 * j2 + 128], ynp[:, 128 * j2:128 * j2 + 128],
                    ident[:, :]), sem="p1_tm", inc=1)
            ytr_done = c.val("p1_tm")
            c.wait("vector", "p1_tm", ytr_done)
            for j2 in range(4):
                c.emit("vector", lambda e, j2=j2, ql=ql: e.scalar_tensor_tensor(
                    out=ynT_sb[j2][:, ql:ql + PT], in0=TRb[:, 128 * j2:128 * j2 + 128],
                    scalar=nw4[:, j2:j2 + 1], in1=sgT[j2][:, ql:ql + PT],
                    op0=ALU.mult, op1=ALU.mult), sem="p1_vm", inc=1)
            c.trb_free = c.val("p1_vm")

        c.sdd_tm_end = c.val("p1_tm")
        c.wait("sync", "p1_vm", c.trb_free)
        for j2 in range(4):
            c.emit("sync", lambda e, j2=j2, i=i: e.dma_start(
                out=io["ynT"][128 * j2:128 * j2 + 128, 512 * i:512 * i + 512],
                in_=ynT_sb[j2][:, :]), sem="p1_sd", inc=16)
        n_sd += 4
        c.wait("vector", "p1_sd", 16 * n_sd)

    c.wait("sync", "p1_sd", 16 * n_sd)


def build_phase1(ntiles=NT512):
    nc = mk_nc()
    io = {
        "xnT_full": nc.dram_tensor("xnT_full", [NC * D, TLOC], BF16,
                                   kind="ExternalInput").ap(),
        "WinT": nc.dram_tensor("WinT", [D, 1288], F32, kind="ExternalInput").ap(),
        "conv_w": nc.dram_tensor("conv_w", [768, 4], F32, kind="ExternalInput").ap(),
        "conv_b": nc.dram_tensor("conv_b", [768], F32, kind="ExternalInput").ap(),
        "dt_bias": nc.dram_tensor("dt_bias", [8], F32, kind="ExternalInput").ap(),
        "A_log": nc.dram_tensor("A_log", [8], F32, kind="ExternalInput").ap(),
        "Dv": nc.dram_tensor("Dv", [8], F32, kind="ExternalInput").ap(),
        "norm_w": nc.dram_tensor("norm_w", [512], F32, kind="ExternalInput").ap(),
        "ynT": nc.dram_tensor("ynT", [512, 8192], BF16, kind="ExternalOutput").ap(),
    }
    c = Ctx(nc)
    emit_phase1(c, io, ntiles=ntiles)
    return c.realize()


def emit_epilogue(c, pref, x_old_ap, sc_ap, x_new_ap, xnT_ap, ident,
                  add_ln_xold=False):
    ld = pref + "ld"
    tp = pref + "tp"
    ev = pref + "ev"
    st = pref + "st"
    xold = [c.sbuf(pref + f"xo{i}", [PT, D], F32) for i in range(2)]
    sc = [c.sbuf(pref + f"sc{i}", [PT, D], F32) for i in range(2)]
    scratch = c.sbuf(pref + "scr", [PT, D], F32)
    scratch2 = c.sbuf(pref + "scr2", [PT, D], F32) if add_ln_xold else None
    xnbf = [c.sbuf(pref + f"xn{i}", [PT, D], BF16) for i in range(2)]
    stats = c.sbuf(pref + "stt", [PT, 4], F32)
    ptr = [c.psum(pref + f"ps{i}", [PT, 512], BF16) for i in range(2)]
    stg = [c.sbuf(pref + f"sg{i}", [PT, 512], BF16) for i in range(2)]

    c.wait("tensor", "gcm", c.val("gcm"))
    n_tp = 0
    n_ev = 0
    last_ev = {0: None, 1: None}
    last_st = {0: None, 1: None}
    for i in range(NTT):
        b = i % 2
        if i >= 2:
            c.wait("sync", "lnv", c.lnv_tile_end)
            if xnT_ap is not None:
                c.wait("vector", tp, 32 * (i - 1))
        c.emit("sync", lambda e, i=i, b=b: e.dma_start(
            out=xold[b][:, :], in_=x_old_ap[i * PT:(i + 1) * PT, :]), sem=ld, inc=16)
        c.emit("sync", lambda e, i=i, b=b: e.dma_start(
            out=sc[b][:, :], in_=sc_ap[i * PT:(i + 1) * PT, :]), sem=ld, inc=16)
        c.wait("vector", ld, 32 * (i + 1))
        if add_ln_xold:
            emit_ln_tile(c, xold[b], scratch2, scratch, stats)
            c.wait("vector", "lnv", c.val("lnv"))
            c.emit("vector", lambda e, b=b: e.tensor_add(
                sc[b][:, :], sc[b][:, :], scratch2[:, :]), sem="lnv", inc=1)
        emit_ln_tile(c, sc[b], scratch, scratch, stats)
        c.wait("vector", "lnv", c.val("lnv"))
        c.emit("vector", lambda e, b=b: e.tensor_add(
            xold[b][:, :], xold[b][:, :], scratch[:, :]), sem="lnv", inc=1)
        c.wait("sync", "lnv", c.val("lnv"))
        c.emit("sync", lambda e, i=i, b=b: e.dma_start(
            out=x_new_ap[i * PT:(i + 1) * PT, :], in_=xold[b][:, :]), sem=st, inc=16)
        if xnT_ap is None:
            c.lnv_tile_end = c.val("lnv")
            continue
        emit_ln_tile(c, xold[b], xnbf[b], scratch, stats)
        c.lnv_tile_end = c.val("lnv")
        c.wait("tensor", "lnv", c.lnv_tile_end)
        for j in range(NDT):
            pb = (j // 4) % 2
            sl = (j % 4) * PT
            if j % 4 == 0 and last_ev[pb] is not None:
                c.wait("tensor", ev, last_ev[pb])
            c.emit("tensor", lambda e, b=b, j=j, pb=pb, sl=sl: e.transpose(
                ptr[pb][:, sl:sl + PT], xnbf[b][:, j * PT:(j + 1) * PT],
                ident[:, :]), sem=tp, inc=1)
            n_tp += 1
            if j % 4 == 3:
                c.wait("vector", tp, n_tp)
                if last_st[pb] is not None:
                    c.wait("vector", st, last_st[pb])
                c.emit("vector", lambda e, pb=pb: e.tensor_copy(
                    stg[pb][:, :], ptr[pb][:, :]), sem=ev, inc=1)
                n_ev += 1
                last_ev[pb] = n_ev
                c.wait("sync", ev, n_ev)
                j0 = j - 3
                c.emit("sync", lambda e, pb=pb, i=i, j0=j0: e.dma_start(
                    out=xnT_ap.rearrange("(j p) t -> p j t", p=PT)
                    [:, j0:j0 + 4, i * PT:(i + 1) * PT],
                    in_=stg[pb][:, :].rearrange("p (j t) -> p j t", t=PT)),
                    sem=st, inc=16)
                last_st[pb] = c.val(st)
    c.wait("sync", st, c.val(st))


def emit_phase1b(c, io):
    ident = c.sbuf("p1b_id", [PT, PT], BF16)
    emit_identity(c, ident)
    ynsb = c.sbuf("p1b_yn", [PT, NDT * TLOC], BF16)
    wsb = c.sbuf("p1b_w", [PT, NDT * 512], BF16)
    exsb = [c.sbuf(f"p1b_ex{i}", [PT, 512], F32) for i in range(2)]
    mmps = [c.psum(f"p1b_ps{i}", [PT, 512], F32) for i in range(2)]

    c.emit("sync", lambda e: e.dma_start(
        out=ynsb[:, :].rearrange("p (j t) -> p j t", t=TLOC),
        in_=io["ynT_own"].rearrange("(j p) t -> p j t", p=PT)), sem="p1b_yl", inc=16)
    ynv = ynsb[:, :].rearrange("p (j t) -> p j t", t=TLOC)
    wv = wsb[:, :].rearrange("p (j q) -> p j q", q=512)

    c.wait("tensor", "p1b_yl", 16)
    n_mm = 0
    n_ev = 0
    n_st = 0
    for dc in range(8):
        if dc > 0:
            c.wait("gpsimd", "p1b_mm", n_mm)
        c.emit("gpsimd", lambda e, dc=dc: e.dma_start(
            out=wv, in_=io["WoutT"][:, 512 * dc:512 * dc + 512]
            .rearrange("(j p) q -> p j q", p=PT)), sem="p1b_wl", inc=16)
        c.wait("tensor", "p1b_wl", 16 * (dc + 1))
        for t in range(NTT):
            pb = t % 2
            if n_mm >= 2:
                c.wait("tensor", "p1b_ev", n_ev - 1)
            for j in range(NDT):
                c.emit("tensor", lambda e, t=t, j=j, pb=pb: e.matmul(
                    mmps[pb][:, :], ynv[:, j, PT * t:PT * t + PT], wv[:, j, :],
                    start=(j == 0), stop=(j == NDT - 1)),
                    sem="p1b_mm", inc=(1 if j == NDT - 1 else 0))
            n_mm += 1
            c.wait("vector", "p1b_mm", n_mm)
            if n_ev >= 2:
                c.wait("vector", "p1b_st", 16 * (n_st - 1))
            c.emit("vector", lambda e, pb=pb: e.tensor_copy(
                exsb[pb][:, :], mmps[pb][:, :]), sem="p1b_ev", inc=1)
            n_ev += 1
            c.wait("sync", "p1b_ev", n_ev)
            c.emit("sync", lambda e, t=t, dc=dc, pb=pb: e.dma_start(
                out=io["sc"][PT * t:PT * t + PT, 512 * dc:512 * dc + 512],
                in_=exsb[pb][:, :]), sem="p1b_st", inc=16)
            n_st += 1
    c.wait("sync", "p1b_st", 16 * n_st)
    emit_epilogue(c, "p1e_", io["x_old"], io["sc"], io["x_new"], io["xnT_next"],
                  ident)


def build_phase1b():
    nc = mk_nc()
    io = {
        "ynT_own": nc.dram_tensor("ynT_own", [D, TLOC], BF16,
                                  kind="ExternalInput").ap(),
        "WoutT": nc.dram_tensor("WoutT", [D, D], F32, kind="ExternalInput").ap(),
        "x_old": nc.dram_tensor("x_old", [TLOC, D], F32, kind="ExternalInput").ap(),
        "sc": nc.dram_tensor("sc", [TLOC, D], F32).ap(),
        "x_new": nc.dram_tensor("x_new", [TLOC, D], F32, kind="ExternalOutput").ap(),
        "xnT_next": nc.dram_tensor("xnT_next", [D, TLOC], BF16,
                                   kind="ExternalOutput").ap(),
    }
    c = Ctx(nc)
    emit_phase1b(c, io)
    return c.realize()


def emit_phase2(c, io, parts="ABCE"):
    c.ccol(EPS)
    c.ccol(1.0)
    ident = c.sbuf("p2_id", [PT, PT], BF16)
    emit_identity(c, ident)
    identf = c.sbuf("p2_idf", [PT, PT], F32)
    c.emit("gpsimd", lambda e: e.memset(identf[:, :], 1.0), sem="gcm", inc=1)
    c.wait("gpsimd", "gcm", c.val("gcm"))
    c.emit("gpsimd", lambda e: e.affine_select(
        out=identf[:, :], in_=identf[:, :], compare_op=ALU.is_equal, fill=0.0,
        base=0, pattern=[[-1, PT]], channel_multiplier=1), sem="gcm", inc=1)
    SELE = c.sbuf("p2_sele", [64, 1024], F32)
    c.emit("gpsimd", lambda e: e.memset(SELE[:, :], 1.0), sem="gcm", inc=1)
    c.wait("gpsimd", "gcm", c.val("gcm"))
    c.emit("gpsimd", lambda e: e.affine_select(
        out=SELE[:, :], in_=SELE[:, :], compare_op=ALU.is_equal, fill=0.0,
        base=0, pattern=[[-1, 64], [0, 16]], channel_multiplier=1), sem="gcm", inc=1)

    xnsb = c.sbuf("p2_xn", [PT, NDT * TLOC], BF16)
    wrsb = c.sbuf("p2_wr", [PT, NDT * 64], BF16)
    atsb = [c.sbuf(f"p2_at{i}", [PT, NDT * PT], BF16) for i in range(2)]
    bpsb = c.sbuf("p2_bp", [PT, 8 * D], BF16)
    hp_all = c.sbuf("p2_hp", [PT, 8 * TLOC], BF16)
    hsb = c.sbuf("p2_hs", [PT, TLOC], BF16)
    ssc = c.sbuf("p2_ssc", [PT, 64], F32)
    vv = c.sbuf("p2_vv", [PT, 64], F32)
    msk = c.sbuf("p2_msk", [PT, 64], F32)
    mxs = c.sbuf("p2_mxs", [PT, 20], F32)
    Gsb = c.sbuf("p2_g", [PT, 64], F32)
    GTsb = c.sbuf("p2_gt", [64, TLOC], F32)
    exsb = [c.sbuf(f"p2_ex{i}", [PT, 512], F32) for i in range(2)]

    mmps = [c.psum(f"p2_ps{i}", [PT, 512], F32) for i in range(2)]
    sc_ps = c.psum("p2_scps", [PT, 64], F32)
    gt_ps = c.psum("p2_gtps", [64, PT], F32)
    gx_ps = c.psum("p2_gxps", [PT, TLOC], F32)
    h_ps = c.psum("p2_hps", [PT, TLOC], F32)

    c.emit("sync", lambda e: e.dma_start(
        out=xnsb[:, :].rearrange("p (j t) -> p j t", t=TLOC),
        in_=io["xnT"].rearrange("(j p) t -> p j t", p=PT)), sem="p2_xl", inc=16)
    c.emit("gpsimd", lambda e: e.dma_start(
        out=wrsb[:, :].rearrange("p (j q) -> p j q", q=64),
        in_=io["WrT"].rearrange("(j p) q -> p j q", p=PT)), sem="p2_wl", inc=16)
    xnv = xnsb[:, :].rearrange("p (j t) -> p j t", t=TLOC)
    wrv = wrsb[:, :].rearrange("p (j q) -> p j q", q=64)

    c.wait("tensor", "p2_xl", 16)
    c.wait("tensor", "p2_wl", 16)
    c.wait("tensor", "gcm", c.val("gcm"))
    c.wait("vector", "gcm", c.val("gcm"))
    for t in range(NTT):
        if t > 0:
            c.wait("tensor", "p2_vm", c.p2_ssc_done)
        for j in range(NDT):
            c.emit("tensor", lambda e, t=t, j=j: e.matmul(
                sc_ps[:, :], xnv[:, j, PT * t:PT * t + PT], wrv[:, j, :],
                start=(j == 0), stop=(j == NDT - 1)),
                sem="p2_tm", inc=(1 if j == NDT - 1 else 0))
        c.wait("vector", "p2_tm", c.val("p2_tm"))
        c.emit("vector", lambda e: e.tensor_copy(ssc[:, :], sc_ps[:, :]),
               sem="p2_vm", inc=1)
        c.p2_ssc_done = c.val("p2_vm")
        c.wait("vector", "p2_vm", c.val("p2_vm"))
        c.emit("vector", lambda e: e.tensor_copy(vv[:, :], ssc[:, :]),
               sem="p2_vm", inc=1)
        for it in range(16):
            c.wait("vector", "p2_vm", c.val("p2_vm"))
            c.emit("vector", lambda e, it=it: e.reduce_max(
                mxs[:, it:it + 1], vv[:, :], axis=mybir.AxisListType.X),
                sem="p2_vm", inc=1)
            if it < 15:
                c.wait("vector", "p2_vm", c.val("p2_vm"))
                c.emit("vector", lambda e, it=it: e.tensor_scalar(
                    out=msk[:, :], in0=vv[:, :], scalar1=mxs[:, it:it + 1],
                    scalar2=None, op0=ALU.is_ge), sem="p2_vm", inc=1)
                c.wait("vector", "p2_vm", c.val("p2_vm"))
                c.emit("vector", lambda e: e.scalar_tensor_tensor(
                    out=vv[:, :], in0=msk[:, :], scalar=-1e30, in1=vv[:, :],
                    op0=ALU.mult, op1=ALU.add), sem="p2_vm", inc=1)
        c.wait("vector", "p2_vm", c.val("p2_vm"))
        c.emit("vector", lambda e: e.tensor_scalar(
            out=mxs[:, 16:17], in0=mxs[:, 0:1], scalar1=-1.0, scalar2=None,
            op0=ALU.mult), sem="p2_vm", inc=1)
        c.emit("vector", lambda e: e.tensor_scalar(
            out=msk[:, :], in0=ssc[:, :], scalar1=mxs[:, 15:16], scalar2=None,
            op0=ALU.is_ge), sem="p2_vm", inc=1)
        c.wait("scalar", "p2_vm", c.val("p2_vm"))
        c.emit("scalar", lambda e: e.activation(
            Gsb[:, :], ssc[:, :], AF.Exp, bias=mxs[:, 16:17], scale=1.0),
            sem="p2_sm", inc=1)
        c.wait("vector", "p2_sm", c.val("p2_sm"))
        c.emit("vector", lambda e: e.tensor_mul(Gsb[:, :], Gsb[:, :], msk[:, :]),
               sem="p2_vm", inc=1)
        c.wait("vector", "p2_vm", c.val("p2_vm"))
        c.emit("vector", lambda e: e.reduce_sum(
            mxs[:, 17:18], Gsb[:, :], axis=mybir.AxisListType.X), sem="p2_vm", inc=1)
        c.wait("vector", "p2_vm", c.val("p2_vm"))
        c.emit("vector", lambda e: e.reciprocal(mxs[:, 18:19], mxs[:, 17:18]),
               sem="p2_vm", inc=1)
        c.wait("vector", "p2_vm", c.val("p2_vm"))
        c.emit("vector", lambda e: e.tensor_scalar(
            out=Gsb[:, :], in0=Gsb[:, :], scalar1=mxs[:, 18:19], scalar2=2.0,
            op0=ALU.mult, op1=ALU.mult), sem="p2_vm", inc=1)
        g_done = c.val("p2_vm")
        c.wait("tensor", "p2_vm", g_done)
        c.emit("tensor", lambda e: e.transpose(
            gt_ps[:, :], Gsb[:, :], identf[:, :]), sem="p2_tm", inc=1)
        c.wait("vector", "p2_tm", c.val("p2_tm"))
        c.emit("vector", lambda e, t=t: e.tensor_copy(
            GTsb[:, PT * t:PT * t + PT], gt_ps[:, :]), sem="p2_vm", inc=1)
        c.p2_gt_done = c.val("p2_vm")

    if "B" not in parts:
        c.wait("sync", "p2_vm", c.p2_gt_done)
        c.emit("sync", lambda e: e.dma_start(
            out=io["sc"][0:64, 0:TLOC], in_=GTsb[:, :]), sem="p2_st", inc=16)
        c.wait("sync", "p2_st", 16)
        return
    for kc in range(8):
        bb = kc % 2
        if kc >= 2:
            c.wait("gpsimd", "p2_tm", c.p2_h_mm[kc - 2])
        c.emit("gpsimd", lambda e, kc=kc: e.dma_start(
            out=atsb[kc % 2][:, :].rearrange("p (j q) -> p j q", q=PT),
            in_=io["AT"][:, PT * kc:PT * kc + PT]
            .rearrange("(j p) q -> p j q", p=PT)), sem="p2_al", inc=16)
        if kc == 0:
            c.wait("tensor", "p2_vm", c.p2_gt_done)
        else:
            c.wait("tensor", "p2_vm", c.p2_hp_done)
        for half in range(2):
            c.emit("tensor", lambda e, kc=kc, half=half: e.matmul(
                gx_ps[:, 512 * half:512 * half + 512],
                SELE[:, PT * kc:PT * kc + PT], GTsb[:, 512 * half:512 * half + 512],
                start=True, stop=True), sem="p2_tm", inc=1)
        c.wait("tensor", "p2_al", 16 * (kc + 1))
        if kc >= 1:
            c.wait("tensor", "p2_vm", c.p2_hev[kc - 1])
        atv = atsb[bb][:, :].rearrange("p (j q) -> p j q", q=PT)
        for half in range(2):
            for j in range(NDT):
                c.emit("tensor", lambda e, j=j, half=half, atv=atv: e.matmul(
                    h_ps[:, 512 * half:512 * half + 512], atv[:, j, :],
                    xnv[:, j, 512 * half:512 * half + 512],
                    start=(j == 0), stop=(j == NDT - 1)),
                    sem="p2_tm", inc=(1 if j == NDT - 1 else 0))
        if not hasattr(c, "p2_h_mm"):
            c.p2_h_mm = {}
            c.p2_hev = {}
        c.p2_h_mm[kc] = c.val("p2_tm")
        c.wait("vector", "p2_tm", c.val("p2_tm"))
        c.emit("vector", lambda e: e.tensor_copy(hsb[:, :], h_ps[:, :]),
               sem="p2_vm", inc=1)
        c.p2_hev[kc] = c.val("p2_vm")
        c.wait("vector", "p2_vm", c.val("p2_vm"))
        c.emit("vector", lambda e, kc=kc: e.tensor_tensor(
            out=hp_all[:, :].rearrange("p (k t) -> p k t", t=TLOC)[:, kc, :],
            in0=hsb[:, :], in1=gx_ps[:, :], op=ALU.mult), sem="p2_vm", inc=1)
        c.p2_hp_done = c.val("p2_vm")

    if "C" not in parts:
        c.wait("sync", "p2_vm", c.p2_hp_done)
        c.emit("sync", lambda e: e.dma_start(
            out=io["sc"].rearrange("t d -> (t d)")[0:PT * 8 * TLOC // 2]
            .rearrange("(p x) -> p x", p=PT), in_=hp_all[:, :].bitcast(F32)),
            sem="p2_st", inc=16)
        c.wait("sync", "p2_st", 16)
        return
    c.emit("gpsimd", lambda e: e.dma_start(
        out=bpsb[:, :].rearrange("p (k q) -> p k q", q=D),
        in_=io["Bp"].rearrange("(k p) q -> p k q", p=PT)), sem="p2_bl", inc=16)
    hpv = hp_all[:, :].rearrange("p (k t) -> p k t", t=TLOC)
    bpv = bpsb[:, :].rearrange("p (k q) -> p k q", q=D)
    c.wait("tensor", "p2_bl", 16)
    c.wait("tensor", "p2_vm", c.p2_hp_done)
    n_ev = 0
    n_st = 0
    last_ev = {0: None, 1: None}
    last_st = {0: None, 1: None}
    for t in range(NTT):
        for dc in range(8):
            pb = (t * 8 + dc) % 2
            if last_ev[pb] is not None:
                c.wait("tensor", "p2_ev", last_ev[pb])
            for kc in range(8):
                c.emit("tensor", lambda e, t=t, dc=dc, kc=kc, pb=pb: e.matmul(
                    mmps[pb][:, :], hpv[:, kc, PT * t:PT * t + PT],
                    bpv[:, kc, 512 * dc:512 * dc + 512],
                    start=(kc == 0), stop=(kc == 7)),
                    sem="p2_tm", inc=(1 if kc == 7 else 0))
            c.wait("vector", "p2_tm", c.val("p2_tm"))
            if last_st[pb] is not None:
                c.wait("vector", "p2_st", last_st[pb])
            c.emit("vector", lambda e, pb=pb: e.tensor_copy(
                exsb[pb][:, :], mmps[pb][:, :]), sem="p2_ev", inc=1)
            n_ev += 1
            last_ev[pb] = n_ev
            c.wait("sync", "p2_ev", n_ev)
            c.emit("sync", lambda e, t=t, dc=dc, pb=pb: e.dma_start(
                out=io["sc"][PT * t:PT * t + PT, 512 * dc:512 * dc + 512],
                in_=exsb[pb][:, :]), sem="p2_st", inc=16)
            n_st += 1
            last_st[pb] = 16 * n_st
    c.wait("sync", "p2_st", 16 * n_st)
    if "E" not in parts:
        return
    c.barrier("p2_bar")
    c.free_psum("p2_hps", "p2_gxps", "p2_gtps", "p2_scps")
    c.free_sbuf("p2_ex1", "p2_ex0", "p2_gt", "p2_g", "p2_mxs", "p2_msk",
                "p2_vv", "p2_ssc", "p2_hs", "p2_hp", "p2_bp", "p2_at1",
                "p2_at0", "p2_wr", "p2_xn")
    emit_epilogue(c, "p2e_", io["x_old"], io["sc"], io["x_new"], io["xnT_next"],
                  ident)


def build_phase2(parts="ABCE"):
    nc = mk_nc()
    io = {
        "xnT": nc.dram_tensor("xnT", [D, TLOC], BF16, kind="ExternalInput").ap(),
        "WrT": nc.dram_tensor("WrT", [D, 64], F32, kind="ExternalInput").ap(),
        "AT": nc.dram_tensor("AT", [D, 1024], F32, kind="ExternalInput").ap(),
        "Bp": nc.dram_tensor("Bp", [1024, D], F32, kind="ExternalInput").ap(),
        "x_old": nc.dram_tensor("x_old", [TLOC, D], F32, kind="ExternalInput").ap(),
        "sc": nc.dram_tensor("sc", [TLOC, D], F32).ap(),
        "x_new": nc.dram_tensor("x_new", [TLOC, D], F32, kind="ExternalOutput").ap(),
        "xnT_next": nc.dram_tensor("xnT_next", [D, TLOC], BF16,
                                   kind="ExternalOutput").ap(),
    }
    c = Ctx(nc)
    emit_phase2(c, io, parts=parts)
    return c.realize()


def emit_phase3(c, io):
    c.ccol(EPS)
    c.ccol(1.0)
    ident = c.sbuf("p3_id", [PT, PT], BF16)
    emit_identity(c, ident)
    ones1 = c.sbuf("p3_ones1", [1, PT], F32)
    c.emit("gpsimd", lambda e: e.memset(ones1[:, :], 1.0), sem="gcm", inc=1)
    onesb = c.sbuf("p3_onesb", [PT, 1], BF16)
    c.emit("gpsimd", lambda e: e.memset(onesb[:, :], 1.0), sem="gcm", inc=1)
    SEL4 = c.sbuf("p3_sel4", [4, 256], F32)
    c.emit("gpsimd", lambda e: e.memset(SEL4[:, :], 1.0), sem="gcm", inc=1)
    c.wait("gpsimd", "gcm", c.val("gcm"))
    c.emit("gpsimd", lambda e: e.affine_select(
        out=SEL4[:, :], in_=SEL4[:, :], compare_op=ALU.is_equal, fill=0.0,
        base=0, pattern=[[-1, 4], [0, 64]], channel_multiplier=1), sem="gcm", inc=1)

    wdT = c.sbuf("p3_wd", [PT, NDT * PT], BF16)
    wuT = c.sbuf("p3_wu", [PT, D], BF16)
    hyw = c.sbuf("p3_hyw", [PT, 552], BF16)
    fcd = c.sbuf("p3_fcd", [PT, 8], BF16)
    ow = c.sbuf("p3_ow", [PT, 2 * PT], BF16)
    bdn = c.sbuf("p3_bdn", [PT, 1], F32)
    bupr = c.sbuf("p3_bup", [1, D], F32)
    bupB = c.sbuf("p3_bupB", [PT, D], F32)
    cwb = c.sbuf("p3_cwb", [PT, 2], F32)
    cbb = c.sbuf("p3_cbb", [PT, 2], F32)
    hyD4 = c.sbuf("p3_hyD", [4, 1], F32)
    hynw = c.sbuf("p3_hynw", [PT, 2], F32)
    g1sb = c.sbuf("p3_g1", [1, 1], F32)
    acol = c.sbuf("p3_acol", [PT, 1], F32)
    apcol = c.sbuf("p3_apcol", [PT, 1], F32)
    omcol = c.sbuf("p3_omcol", [PT, 1], F32)

    c.emit("gpsimd", lambda e: e.dma_start(
        out=wdT[:, :].rearrange("p (j q) -> p j q", q=PT),
        in_=io["w_downT"].rearrange("(j p) q -> p j q", p=PT)), sem="p3_wl", inc=16)
    c.emit("gpsimd", lambda e: e.dma_start(
        out=wuT[:, :], in_=io["w_upT"]), sem="p3_wl", inc=16)
    c.emit("gpsimd", lambda e: e.dma_start(
        out=hyw[:, :], in_=io["hy_in_wT"]), sem="p3_wl", inc=16)
    c.emit("gpsimd", lambda e: e.dma_start(
        out=fcd[:, :].rearrange("p (j q) -> p j q", q=4),
        in_=io["hy_fcD_wT"].rearrange("(j p) q -> p j q", p=PT)), sem="p3_wl", inc=16)
    c.emit("gpsimd", lambda e: e.dma_start(
        out=ow[:, :].rearrange("p (j q) -> p j q", q=PT),
        in_=io["hy_out_wT"].rearrange("(j p) q -> p j q", p=PT)), sem="p3_wl", inc=16)
    c.emit("sync", lambda e: e.dma_start(
        out=bdn[:, :], in_=io["b_down"].unsqueeze(1)), sem="p3_pl", inc=16)
    c.emit("sync", lambda e: e.dma_start(
        out=bupr[:, :], in_=io["b_up"].unsqueeze(0)), sem="p3_pl", inc=16)
    c.emit("sync", lambda e: e.dma_start(
        out=cwb[:, :], in_=io["cw_xs"].rearrange("(j p) -> p j", p=PT),
        allow_slow_non_contiguous=True), sem="p3_pl", inc=16)
    c.emit("sync", lambda e: e.dma_start(
        out=cbb[:, :], in_=io["cb_xs"].rearrange("(j p) -> p j", p=PT),
        allow_slow_non_contiguous=True), sem="p3_pl", inc=16)
    c.emit("sync", lambda e: e.dma_start(
        out=hyD4[:, :], in_=io["hy_D"].unsqueeze(1)), sem="p3_pl", inc=16)
    c.emit("sync", lambda e: e.dma_start(
        out=hynw[:, :], in_=io["hy_norm_w"].rearrange("(j p) -> p j", p=PT),
        allow_slow_non_contiguous=True), sem="p3_pl", inc=16)
    c.emit("sync", lambda e: e.dma_start(
        out=g1sb[:, :], in_=io["gate1"].unsqueeze(0)), sem="p3_pl", inc=16)

    xnsb = c.sbuf("p3_xn", [PT, NDT * TLOC], BF16)
    c.emit("sync", lambda e: e.dma_start(
        out=xnsb[:, :].rearrange("p (j t) -> p j t", t=TLOC),
        in_=io["xnT"].rearrange("(j p) t -> p j t", p=PT)), sem="p3_xl", inc=16)
    xnv = xnsb[:, :].rearrange("p (j t) -> p j t", t=TLOC)

    c.wait("scalar", "p3_pl", 112)
    c.emit("scalar", lambda e: e.activation(
        g1sb[:, :], g1sb[:, :], AF.Sigmoid, scale=1.0), sem="p3_sm", inc=1)
    pA = c.psum("p3_pA", [PT, 512], F32)
    pB = c.psum("p3_pB", [PT, 512], F32)
    pX = c.psum("p3_pX", [PT, 512], F32)
    pH = c.psum("p3_pH", [PT, 512], F32)
    c.wait("tensor", "p3_sm", c.val("p3_sm"))
    c.wait("tensor", "gcm", c.val("gcm"))
    c.emit("tensor", lambda e: e.matmul(
        pX[:, 0:1], ones1[0:1, :], g1sb[:, :], start=True, stop=True),
        sem="p3_tm", inc=1)
    c.wait("vector", "p3_tm", c.val("p3_tm"))
    c.emit("vector", lambda e: e.tensor_copy(acol[:, :], pX[:, 0:1]),
           sem="p3_vm", inc=1)
    c.wait("vector", "p3_vm", c.val("p3_vm"))
    c.emit("vector", lambda e: e.tensor_scalar(
        out=omcol[:, :], in0=acol[:, :], scalar1=-1.0, scalar2=1.0,
        op0=ALU.mult, op1=ALU.add), sem="p3_vm", inc=1)
    c.emit("vector", lambda e: e.tensor_scalar(
        out=apcol[:, :], in0=acol[:, :], scalar1=1.0, scalar2=None,
        op0=ALU.add), sem="p3_vm", inc=1)
    c.wait("tensor", "p3_pl", 112)
    c.wait("tensor", "p3_vm", c.val("p3_vm"))
    for dc in range(8):
        c.emit("tensor", lambda e, dc=dc: e.matmul(
            pA[:, :], ones1[0:1, :], bupr[0:1, 512 * dc:512 * dc + 512],
            start=True, stop=True), sem="p3_tm", inc=1)
        c.wait("vector", "p3_tm", c.val("p3_tm"))
        c.emit("vector", lambda e, dc=dc: e.tensor_copy(
            bupB[:, 512 * dc:512 * dc + 512], pA[:, :]), sem="p3_vm", inc=1)
        c.wait("tensor", "p3_vm", c.val("p3_vm"))

    xmf = c.sbuf("p3_xmf", [PT, 512], F32)
    xmb = c.sbuf("p3_xmb", [PT, 512], BF16)
    u2 = c.sbuf("p3_u2", [PT, 512], BF16)
    sz = [c.sbuf(f"p3_sz{j}", [PT, 512], F32) for j in range(2)]
    xsb = [c.sbuf(f"p3_xs{j}", [PT, 512], BF16) for j in range(2)]
    v4sb = c.sbuf("p3_v4", [4, 512], F32)
    tmp = c.sbuf("p3_tmp", [PT, 512], F32)
    yz = [c.sbuf(f"p3_yz{j}", [PT, 512], F32) for j in range(2)]
    sqb = c.sbuf("p3_sqb", [PT, 512], BF16)
    rstd = c.sbuf("p3_rstd", [1, 512], F32)
    ynT = [c.sbuf(f"p3_yn{j}", [PT, 512], BF16) for j in range(2)]
    hfT = c.sbuf("p3_hf", [PT, 512], BF16)
    exsb = [c.sbuf(f"p3_ex{i}", [PT, 512], F32) for i in range(2)]
    eps_col = c.ccol(EPS)

    def hydra(uT, out_ps, tag):
        for pc in range(4):
            ps = [pA, pB][pc % 2]
            fs, fv = c.p3_zx_free[pc % 2]
            c.wait("tensor", fs, fv)
            c.emit("tensor", lambda e, pc=pc, ps=ps, uT=uT: e.matmul(
                ps[:, :], hyw[:, 128 * pc:128 * pc + 128], uT[:, :],
                start=True, stop=True), sem="p3_tm", inc=1)
            mmv = c.val("p3_tm")
            c.wait("scalar", "p3_tm", mmv)
            if pc < 2:
                c.emit("scalar", lambda e, pc=pc, ps=ps: e.activation(
                    sz[pc][:, :], ps[:, :], AF.Silu, scale=1.0), sem="p3_vs", inc=1)
            else:
                c.emit("scalar", lambda e, pc=pc, ps=ps: e.activation(
                    xsb[pc - 2][:, :], ps[:, :], AF.Silu,
                    scale=cwb[:, pc - 2:pc - 1], bias=cbb[:, pc - 2:pc - 1]),
                    sem="p3_vs", inc=1)
            c.p3_zx_free[pc % 2] = ("p3_vs", c.val("p3_vs"))
        c.wait("tensor", "p3_vs", c.val("p3_vs"))
        c.wait("tensor", "p3_vm", c.p3_px_free)
        for j in range(2):
            c.emit("tensor", lambda e, j=j: e.matmul(
                pX[0:4, :], fcd[:, 4 * j:4 * j + 4], xsb[j][:, :],
                start=(j == 0), stop=(j == 1)), sem="p3_tm", inc=(1 if j else 0))
        c.wait("vector", "p3_tm", c.val("p3_tm"))
        c.emit("vector", lambda e: e.tensor_scalar(
            out=v4sb[:, :], in0=pX[0:4, :], scalar1=hyD4[:, 0:1], scalar2=None,
            op0=ALU.add), sem="p3_vm", inc=1)
        v4_done = c.val("p3_vm")
        for j in range(2):
            c.wait("tensor", "p3_vm", c.val("p3_vm"))
            c.emit("tensor", lambda e, j=j: e.matmul(
                pX[:, :], SEL4[:, 128 * j:128 * j + 128], v4sb[:, :],
                start=True, stop=True), sem="p3_tm", inc=1)
            c.wait("vector", "p3_tm", c.val("p3_tm"))
            c.emit("vector", lambda e, j=j: e.tensor_tensor(
                out=tmp[:, :], in0=xsb[j][:, :], in1=pX[:, :], op=ALU.mult),
                sem="p3_vm", inc=1)
            c.wait("vector", "p3_vm", c.val("p3_vm"))
            c.emit("vector", lambda e, j=j: e.tensor_mul(
                yz[j][:, :], tmp[:, :], sz[j][:, :]), sem="p3_vm", inc=1)
        for j in range(2):
            c.wait("scalar", "p3_vm", c.val("p3_vm"))
            c.wait("scalar", "p3_tm", c.val("p3_tm"))
            c.emit("scalar", lambda e, j=j: e.activation(
                sqb[:, :], yz[j][:, :], AF.Square, scale=1.0), sem="p3_vs", inc=1)
            c.wait("tensor", "p3_vs", c.val("p3_vs"))
            c.emit("tensor", lambda e, j=j: e.matmul(
                pX[0:1, :], onesb[:, 0:1], sqb[:, :],
                start=(j == 0), stop=(j == 1)), sem="p3_tm", inc=(1 if j else 0))
        c.wait("scalar", "p3_tm", c.val("p3_tm"))
        c.emit("scalar", lambda e: e.activation(
            rstd[:, :], pX[0:1, :], AF.Ln, bias=eps_col[0:1, 0:1], scale=1.0 / 256),
            sem="p3_vs", inc=1)
        c.wait("scalar", "p3_vs", c.val("p3_vs"))
        c.emit("scalar", lambda e: e.activation(
            rstd[:, :], rstd[:, :], AF.Exp, scale=-0.5), sem="p3_vs", inc=1)
        c.wait("tensor", "p3_vs", c.val("p3_vs"))
        c.emit("tensor", lambda e: e.matmul(
            pX[:, :], ones1[0:1, :], rstd[:, :], start=True, stop=True),
            sem="p3_tm", inc=1)
        rsb_done = c.val("p3_tm")
        c.wait("vector", "p3_tm", rsb_done)
        for j in range(2):
            c.emit("vector", lambda e, j=j: e.scalar_tensor_tensor(
                out=ynT[j][:, :], in0=pX[:, :], scalar=hynw[:, j:j + 1],
                in1=yz[j][:, :], op0=ALU.mult, op1=ALU.mult), sem="p3_vm", inc=1)
        yn_done = c.val("p3_vm")
        c.p3_px_free = yn_done
        c.wait("tensor", "p3_vm", yn_done)
        for j in range(2):
            c.emit("tensor", lambda e, j=j, out_ps=out_ps: e.matmul(
                out_ps[:, :], ow[:, 128 * j:128 * j + 128], ynT[j][:, :],
                start=(j == 0), stop=(j == 1)), sem="p3_tm", inc=(1 if j else 0))
        return c.val("p3_tm")

    c.wait("tensor", "p3_wl", 80)
    c.wait("tensor", "p3_xl", 16)
    n_st = 0
    last_ev = {0: None, 1: None}
    last_st = {0: None, 1: None}
    c.p3_zx_free = {0: ("p3_vs", 0), 1: ("p3_vs", 0)}
    c.p3_px_free = 0
    for s in range(2):
        t0 = 512 * s
        if s > 0:
            c.wait("tensor", "p3_vs", c.p3_xm_done)
        for j in range(NDT):
            c.emit("tensor", lambda e, j=j, t0=t0: e.matmul(
                pH[:, :], wdT[:, :].rearrange("p (j q) -> p j q", q=PT)[:, j, :],
                xnv[:, j, t0:t0 + 512], start=(j == 0), stop=(j == NDT - 1)),
                sem="p3_tm", inc=(1 if j == NDT - 1 else 0))
        c.wait("scalar", "p3_tm", c.val("p3_tm"))
        c.emit("scalar", lambda e: e.activation(
            xmf[:, :], pH[:, :], AF.Gelu, bias=bdn[:, 0:1], scale=1.0),
            sem="p3_vs", inc=1)
        c.wait("vector", "p3_vs", c.val("p3_vs"))
        c.emit("vector", lambda e: e.tensor_copy(xmb[:, :], xmf[:, :]),
               sem="p3_vm", inc=1)
        c.p3_xm_done = c.val("p3_vs")
        xm_v = c.val("p3_vm")
        c.wait("tensor", "p3_vm", xm_v)
        h1 = hydra(xmb, pH, "h1")
        c.wait("vector", "p3_tm", h1)
        c.emit("vector", lambda e: e.tensor_copy(u2[:, :], pH[:, :]),
               sem="p3_vm", inc=1)
        c.wait("tensor", "p3_vm", c.val("p3_vm"))
        h2 = hydra(u2, pH, "h2")
        c.wait("vector", "p3_tm", h2)
        c.emit("vector", lambda e: e.tensor_scalar(
            out=tmp[:, :], in0=xmf[:, :], scalar1=apcol[:, 0:1], scalar2=None,
            op0=ALU.mult), sem="p3_vm", inc=1)
        c.wait("vector", "p3_vm", c.val("p3_vm"))
        c.emit("vector", lambda e: e.scalar_tensor_tensor(
            out=hfT[:, :], in0=pH[:, :], scalar=omcol[:, 0:1], in1=tmp[:, :],
            op0=ALU.mult, op1=ALU.add), sem="p3_vm", inc=1)
        hf_done = c.val("p3_vm")
        c.wait("tensor", "p3_vm", hf_done)
        for tt in range(4):
            for dc in range(8):
                pb = (tt * 8 + dc) % 2
                ps = [pA, pB][pb]
                if last_ev[pb] is not None:
                    c.wait("tensor", "p3_ev", last_ev[pb])
                c.emit("tensor", lambda e, tt=tt, dc=dc, ps=ps: e.matmul(
                    ps[:, :], hfT[:, 128 * tt:128 * tt + 128],
                    wuT[:, 512 * dc:512 * dc + 512], start=True, stop=True),
                    sem="p3_tm", inc=1)
                c.wait("vector", "p3_tm", c.val("p3_tm"))
                if last_st[pb] is not None:
                    c.wait("vector", "p3_st", last_st[pb])
                c.emit("vector", lambda e, dc=dc, pb=pb, ps=ps: e.tensor_tensor(
                    out=exsb[pb][:, :], in0=ps[:, :],
                    in1=bupB[:, 512 * dc:512 * dc + 512], op=ALU.add),
                    sem="p3_ev", inc=1)
                last_ev[pb] = c.val("p3_ev")
                c.wait("sync", "p3_ev", c.val("p3_ev"))
                c.emit("sync", lambda e, s=s, tt=tt, dc=dc, pb=pb: e.dma_start(
                    out=io["sc"][512 * s + 128 * tt:512 * s + 128 * tt + 128,
                                 512 * dc:512 * dc + 512],
                    in_=exsb[pb][:, :]), sem="p3_st", inc=16)
                n_st += 1
                last_st[pb] = 16 * n_st
        c.p3_zx_free = {0: ("p3_ev", last_ev[0] or 0),
                        1: ("p3_ev", last_ev[1] or 0)}
        c.wait("tensor", "p3_vm", hf_done)
    c.wait("sync", "p3_st", 16 * n_st)
    c.barrier("p3_bar")
    c.free_sbuf("p3_ex1", "p3_ex0", "p3_hf", "p3_yn1", "p3_yn0", "p3_rstd",
                "p3_sqb", "p3_yz1", "p3_yz0", "p3_tmp", "p3_v4", "p3_xs1",
                "p3_xs0", "p3_sz1", "p3_sz0", "p3_u2", "p3_xmb", "p3_xmf",
                "p3_xn", "p3_omcol", "p3_apcol", "p3_acol", "p3_g1", "p3_hynw",
                "p3_hyD", "p3_cbb", "p3_cwb", "p3_bupB", "p3_bup", "p3_bdn",
                "p3_ow", "p3_fcd", "p3_hyw", "p3_wu", "p3_wd")
    c.free_psum("p3_pH", "p3_pX", "p3_pB", "p3_pA")
    emit_epilogue(c, "p3e_", io["x_old"], io["sc"], io["x_new"], io["xnT_next"],
                  ident, add_ln_xold=True)


def build_phase3():
    nc = mk_nc()
    io = {
        "xnT": nc.dram_tensor("xnT", [D, TLOC], BF16, kind="ExternalInput").ap(),
        "w_downT": nc.dram_tensor("w_downT", [D, PT], F32, kind="ExternalInput").ap(),
        "b_down": nc.dram_tensor("b_down", [PT], F32, kind="ExternalInput").ap(),
        "w_upT": nc.dram_tensor("w_upT", [PT, D], F32, kind="ExternalInput").ap(),
        "b_up": nc.dram_tensor("b_up", [D], F32, kind="ExternalInput").ap(),
        "gate1": nc.dram_tensor("gate1", [1], F32, kind="ExternalInput").ap(),
        "hy_in_wT": nc.dram_tensor("hy_in_wT", [PT, 552], F32,
                                   kind="ExternalInput").ap(),
        "cw_xs": nc.dram_tensor("cw_xs", [256], F32, kind="ExternalInput").ap(),
        "cb_xs": nc.dram_tensor("cb_xs", [256], F32, kind="ExternalInput").ap(),
        "hy_D": nc.dram_tensor("hy_D", [4], F32, kind="ExternalInput").ap(),
        "hy_fcD_wT": nc.dram_tensor("hy_fcD_wT", [256, 4], F32,
                                    kind="ExternalInput").ap(),
        "hy_norm_w": nc.dram_tensor("hy_norm_w", [256], F32,
                                    kind="ExternalInput").ap(),
        "hy_out_wT": nc.dram_tensor("hy_out_wT", [256, PT], F32,
                                    kind="ExternalInput").ap(),
        "x_old": nc.dram_tensor("x_old", [TLOC, D], F32, kind="ExternalInput").ap(),
        "sc": nc.dram_tensor("sc", [TLOC, D], F32).ap(),
        "x_new": nc.dram_tensor("x_new", [TLOC, D], F32, kind="ExternalOutput").ap(),
        "xnT_next": nc.dram_tensor("xnT_next", [D, TLOC], BF16,
                                   kind="ExternalOutput").ap(),
    }
    c = Ctx(nc)
    emit_phase3(c, io)
    return c.realize()


REV_INV = [0, 2, 4, 6, 1, 3, 5, 7]


def emit_phaseD(c, io):
    ones1 = c.sbuf("pd_ones1", [1, PT], F32)
    c.emit("gpsimd", lambda e: e.memset(ones1[:, :], 1.0), sem="gcm", inc=1)
    zs_d = c.nc.dram_tensor("pd_zs", [32, 1408], F32).ap()
    zsd_d = c.nc.dram_tensor("pd_zsd", [11, D], F32).ap()

    qsb = c.sbuf("pd_q", [PT, 32 * 176], I32)
    qz = c.sbuf("pd_qz", [32, 512], I32)
    znib = c.sbuf("pd_zn", [32, 1408], I32)
    zss = c.sbuf("pd_zss", [32, 1408], F32)
    ssb = c.sbuf("pd_ss", [32, 1408], F32)
    srow = c.sbuf("pd_srow", [1, 4096], F32)
    zrow = c.sbuf("pd_zrow", [1, 4096], F32)
    wnib = c.sbuf("pd_wn", [PT, 1408], I32)
    wtmp = c.sbuf("pd_wt", [PT, 1408], F32)
    wout = c.sbuf("pd_wo", [PT, 4096], BF16)
    sps = c.psum("pd_sps", [PT, 1408], F32)
    zps = c.psum("pd_zps", [PT, 1408], F32)

    def unpack_zeros(qz_ap, w_words, sem_post):
        rows = qz_ap.shape[0]
        c.emit("sync", lambda e: e.dma_start(
            out=qz[0:rows, 0:w_words], in_=qz_ap), sem="pd_zl", inc=16)
        c.wait("vector", "pd_zl", c.val("pd_zl"))
        for j in range(8):
            c.emit("vector", lambda e, j=j, rows=rows, w=w_words: e.tensor_scalar(
                out=znib[0:rows, :].rearrange("r (w k) -> r w k", k=8)
                [:, 0:w, REV_INV[j]:REV_INV[j] + 1],
                in0=qz[0:rows, 0:w].unsqueeze(2), scalar1=4 * j, scalar2=15,
                op0=ALU.logical_shift_right, op1=ALU.bitwise_and),
                sem=sem_post, inc=(1 if j == 7 else 0))

    for mat, qw_ap, sc_ap, qz_ap, out_ap in (
            ("g", io["g_qw"], io["g_sc"], io["g_qz"], io["Wg_c"]),
            ("u", io["u_qw"], io["u_sc"], io["u_qz"], io["Wu_c"])):
        c.wait("sync", "pd_vm", c.val("pd_vm"))
        unpack_zeros(qz_ap, 176, "pd_vm")
        c.emit("sync", lambda e, sc_ap=sc_ap: e.dma_start(
            out=ssb[:, :], in_=sc_ap), sem="pd_sl", inc=16)
        c.wait("vector", "pd_sl", c.val("pd_sl"))
        c.wait("vector", "pd_vm", c.val("pd_vm"))
        c.wait("vector", "pd_zb", c.val("pd_zb"))
        c.emit("vector", lambda e: e.tensor_mul(zss[:, :], znib[:, :], ssb[:, :]),
               sem="pd_vm", inc=1)
        c.wait("sync", "pd_vm", c.val("pd_vm"))
        c.emit("sync", lambda e: e.dma_start(out=zs_d, in_=zss[:, :]),
               sem="pd_zb", inc=16)
        c.wait("sync", "pd_tm", c.val("pd_tm"))
        c.wait("sync", "pd_vm2", c.val("pd_vm2"))
        c.emit("sync", lambda e, qw_ap=qw_ap: e.dma_start(
            out=qsb[:, :].rearrange("p (r w) -> p r w", w=176),
            in_=qw_ap.rearrange("(r p) w -> p r w", p=PT)), sem="pd_ql", inc=16)
        for rt in range(32):
            c.wait("sync", "pd_bc", c.val("pd_bc"))
            if rt == 0:
                c.wait("sync", "pd_zb", c.val("pd_zb"))
            c.emit("sync", lambda e, sc_ap=sc_ap, rt=rt: e.dma_start(
                out=srow[0:1, 0:1408], in_=sc_ap[rt:rt + 1, :]), sem="pd_rl", inc=16)
            c.emit("sync", lambda e, rt=rt: e.dma_start(
                out=zrow[0:1, 0:1408], in_=zs_d[rt:rt + 1, :]), sem="pd_rl", inc=16)
            c.wait("tensor", "pd_rl", c.val("pd_rl"))
            c.wait("tensor", "pd_vm2", c.val("pd_vm2"))
            c.wait("tensor", "gcm", c.val("gcm"))
            for seg in range(3):
                s0, s1 = 512 * seg, min(512 * seg + 512, 1408)
                c.emit("tensor", lambda e, s0=s0, s1=s1: e.matmul(
                    sps[:, s0:s1], ones1[0:1, :], srow[0:1, s0:s1],
                    start=True, stop=True), sem="pd_bc", inc=0)
                c.emit("tensor", lambda e, s0=s0, s1=s1: e.matmul(
                    zps[:, s0:s1], ones1[0:1, :], zrow[0:1, s0:s1],
                    start=True, stop=True), sem="pd_bc", inc=(1 if s1 == 1408 else 0))
            c.wait("vector", "pd_ql", c.val("pd_ql"))
            c.wait("vector", "pd_st", c.val("pd_st"))
            for j in range(8):
                c.emit("vector", lambda e, rt=rt, j=j: e.tensor_scalar(
                    out=wnib[:, :].rearrange("p (w k) -> p w k", k=8)
                    [:, :, REV_INV[j]:REV_INV[j] + 1],
                    in0=qsb[:, :].rearrange("p (r w) -> p r w", w=176)
                    [:, rt, :].unsqueeze(2),
                    scalar1=4 * j, scalar2=15,
                    op0=ALU.logical_shift_right, op1=ALU.bitwise_and),
                    sem="pd_vm2", inc=0)
            c.wait("vector", "pd_bc", c.val("pd_bc"))
            c.emit("vector", lambda e: e.tensor_tensor(
                out=wtmp[:, :], in0=wnib[:, :], in1=sps[:, :], op=ALU.mult),
                sem="pd_vm2", inc=0)
            c.emit("vector", lambda e: e.tensor_tensor(
                out=wout[:, 0:1408], in0=wtmp[:, :], in1=zps[:, :],
                op=ALU.subtract), sem="pd_vm2", inc=1)
            c.wait("sync", "pd_vm2", c.val("pd_vm2"))
            c.emit("sync", lambda e, out_ap=out_ap, rt=rt: e.dma_start(
                out=out_ap[PT * rt:PT * rt + PT, :], in_=wout[:, 0:1408]),
                sem="pd_st", inc=16)

    znibd = c.sbuf("pd_znd", [11, D], I32)
    dsc = c.sbuf("pd_dsc", [11, D], F32)
    zsd = c.sbuf("pd_zsd2", [11, D], F32)
    c.wait("sync", "pd_vm", c.val("pd_vm"))
    c.emit("sync", lambda e: e.dma_start(
        out=qz[0:11, 0:512], in_=io["d_qz"]), sem="pd_zl", inc=16)
    c.emit("sync", lambda e: e.dma_start(out=dsc[:, :], in_=io["d_sc"]),
           sem="pd_sl", inc=16)
    c.wait("vector", "pd_zl", c.val("pd_zl"))
    for j in range(8):
        c.emit("vector", lambda e, j=j: e.tensor_scalar(
            out=znibd[:, :].rearrange("r (w k) -> r w k", k=8)
            [:, :, REV_INV[j]:REV_INV[j] + 1],
            in0=qz[0:11, 0:512].unsqueeze(2), scalar1=4 * j, scalar2=15,
            op0=ALU.logical_shift_right, op1=ALU.bitwise_and),
            sem="pd_vm", inc=(1 if j == 7 else 0))
    c.wait("vector", "pd_sl", c.val("pd_sl"))
    c.emit("vector", lambda e: e.tensor_mul(zsd[:, :], znibd[:, :], dsc[:, :]),
           sem="pd_vm", inc=1)
    c.wait("sync", "pd_vm", c.val("pd_vm"))
    c.emit("sync", lambda e: e.dma_start(out=zsd_d, in_=zsd[:, :]),
           sem="pd_zb", inc=16)
    qd = c.sbuf("pd_qd", [PT, 11 * 512], I32)
    c.emit("sync", lambda e: e.dma_start(
        out=qd[:, :].rearrange("p (r w) -> p r w", w=512),
        in_=io["d_qw"].rearrange("(r p) w -> p r w", p=PT)), sem="pd_ql", inc=16)
    for rt in range(11):
        for sub in range(4):
            w0 = 128 * sub
            col0 = 1024 * sub
            c.wait("sync", "pd_bc", c.val("pd_bc"))
            if rt == 0 and sub == 0:
                c.wait("sync", "pd_zb", c.val("pd_zb"))
            c.emit("sync", lambda e, rt=rt, col0=col0: e.dma_start(
                out=srow[0:1, 0:1024], in_=io["d_sc"][rt:rt + 1, col0:col0 + 1024]),
                sem="pd_rl", inc=16)
            c.emit("sync", lambda e, rt=rt, col0=col0: e.dma_start(
                out=zrow[0:1, 0:1024], in_=zsd_d[rt:rt + 1, col0:col0 + 1024]),
                sem="pd_rl", inc=16)
            c.wait("tensor", "pd_rl", c.val("pd_rl"))
            c.wait("tensor", "pd_vm2", c.val("pd_vm2"))
            for seg in range(2):
                s0 = 512 * seg
                c.emit("tensor", lambda e, s0=s0: e.matmul(
                    sps[:, s0:s0 + 512], ones1[0:1, :], srow[0:1, s0:s0 + 512],
                    start=True, stop=True), sem="pd_bc", inc=0)
                c.emit("tensor", lambda e, s0=s0: e.matmul(
                    zps[:, s0:s0 + 512], ones1[0:1, :], zrow[0:1, s0:s0 + 512],
                    start=True, stop=True), sem="pd_bc", inc=(1 if seg else 0))
            c.wait("vector", "pd_ql", c.val("pd_ql"))
            if sub == 0:
                c.wait("vector", "pd_st", c.val("pd_st"))
            for j in range(8):
                c.emit("vector", lambda e, rt=rt, w0=w0, j=j: e.tensor_scalar(
                    out=wnib[:, 0:1024].rearrange("p (w k) -> p w k", k=8)
                    [:, :, REV_INV[j]:REV_INV[j] + 1],
                    in0=qd[:, :].rearrange("p (r w) -> p r w", w=512)
                    [:, rt, w0:w0 + 128].unsqueeze(2),
                    scalar1=4 * j, scalar2=15,
                    op0=ALU.logical_shift_right, op1=ALU.bitwise_and),
                    sem="pd_vm2", inc=0)
            c.wait("vector", "pd_bc", c.val("pd_bc"))
            c.emit("vector", lambda e: e.tensor_tensor(
                out=wtmp[:, 0:1024], in0=wnib[:, 0:1024], in1=sps[:, 0:1024],
                op=ALU.mult), sem="pd_vm2", inc=0)
            c.emit("vector", lambda e, col0=col0: e.tensor_tensor(
                out=wout[:, col0:col0 + 1024], in0=wtmp[:, 0:1024],
                in1=zps[:, 0:1024], op=ALU.subtract), sem="pd_vm2", inc=1)
        c.wait("sync", "pd_vm2", c.val("pd_vm2"))
        c.emit("sync", lambda e, rt=rt: e.dma_start(
            out=io["Wd_c"][PT * rt:PT * rt + PT, :], in_=wout[:, :]),
            sem="pd_st", inc=16)
    c.wait("sync", "pd_st", c.val("pd_st"))


def build_phaseD():
    nc = mk_nc()
    io = {
        "g_qw": nc.dram_tensor("g_qw", [D, 176], I32, kind="ExternalInput").ap(),
        "g_sc": nc.dram_tensor("g_sc", [32, 1408], F32, kind="ExternalInput").ap(),
        "g_qz": nc.dram_tensor("g_qz", [32, 176], I32, kind="ExternalInput").ap(),
        "u_qw": nc.dram_tensor("u_qw", [D, 176], I32, kind="ExternalInput").ap(),
        "u_sc": nc.dram_tensor("u_sc", [32, 1408], F32, kind="ExternalInput").ap(),
        "u_qz": nc.dram_tensor("u_qz", [32, 176], I32, kind="ExternalInput").ap(),
        "d_qw": nc.dram_tensor("d_qw", [1408, 512], I32, kind="ExternalInput").ap(),
        "d_sc": nc.dram_tensor("d_sc", [11, D], F32, kind="ExternalInput").ap(),
        "d_qz": nc.dram_tensor("d_qz", [11, 512], I32, kind="ExternalInput").ap(),
        "Wg_c": nc.dram_tensor("Wg_c", [D, 1408], BF16, kind="ExternalOutput").ap(),
        "Wu_c": nc.dram_tensor("Wu_c", [D, 1408], BF16, kind="ExternalOutput").ap(),
        "Wd_c": nc.dram_tensor("Wd_c", [1408, D], BF16, kind="ExternalOutput").ap(),
    }
    c = Ctx(nc)
    emit_phaseD(c, io)
    return c.realize()


def emit_phase4(c, io):
    c.ccol(EPS)
    c.ccol(1.0)
    ident = c.sbuf("p4_id", [PT, PT], BF16)
    emit_identity(c, ident)
    xnsb = c.sbuf("p4_xn", [PT, NDT * TLOC], BF16)
    wgt = [c.sbuf(f"p4_wg{i}", [PT, NDT * PT], BF16) for i in range(2)]
    wut = [c.sbuf(f"p4_wu{i}", [PT, NDT * PT], BF16) for i in range(2)]
    sgf = c.sbuf("p4_sg", [PT, TLOC], F32)
    hts = c.sbuf("p4_ht", [PT, TLOC], BF16)
    mmps = [c.psum(f"p4_ps{i}", [PT, 512], F32) for i in range(2)]
    g_ps = c.psum("p4_gps", [PT, TLOC], F32)
    u_ps = c.psum("p4_ups", [PT, TLOC], F32)

    c.emit("sync", lambda e: e.dma_start(
        out=xnsb[:, :].rearrange("p (j t) -> p j t", t=TLOC),
        in_=io["xnT"].rearrange("(j p) t -> p j t", p=PT)), sem="p4_xl", inc=16)
    xnv = xnsb[:, :].rearrange("p (j t) -> p j t", t=TLOC)

    c.wait("tensor", "p4_xl", 16)
    for cc in range(88):
        bb = cc % 2
        if cc >= 2:
            c.wait("sync", "p4_tm", c.p4_mm[cc - 2])
        c.emit("sync", lambda e, cc=cc, bb=bb: e.dma_start(
            out=wgt[bb][:, :].rearrange("p (j q) -> p j q", q=PT),
            in_=io["Wg"][:, PT * cc:PT * cc + PT]
            .rearrange("(j p) q -> p j q", p=PT)), sem="p4_wl", inc=16)
        c.emit("sync", lambda e, cc=cc, bb=bb: e.dma_start(
            out=wut[bb][:, :].rearrange("p (j q) -> p j q", q=PT),
            in_=io["Wu"][:, PT * cc:PT * cc + PT]
            .rearrange("(j p) q -> p j q", p=PT)), sem="p4_wl", inc=16)
        c.wait("tensor", "p4_wl", 32 * (cc + 1))
        if cc >= 1:
            c.wait("tensor", "p4_vm", c.p4_h_done)
        wgv = wgt[bb][:, :].rearrange("p (j q) -> p j q", q=PT)
        wuv = wut[bb][:, :].rearrange("p (j q) -> p j q", q=PT)
        for half in range(2):
            for j in range(NDT):
                c.emit("tensor", lambda e, j=j, half=half, wgv=wgv: e.matmul(
                    g_ps[:, 512 * half:512 * half + 512], wgv[:, j, :],
                    xnv[:, j, 512 * half:512 * half + 512],
                    start=(j == 0), stop=(j == NDT - 1)), sem="p4_tm", inc=0)
        for half in range(2):
            for j in range(NDT):
                c.emit("tensor", lambda e, j=j, half=half, wuv=wuv: e.matmul(
                    u_ps[:, 512 * half:512 * half + 512], wuv[:, j, :],
                    xnv[:, j, 512 * half:512 * half + 512],
                    start=(j == 0), stop=(j == NDT - 1)),
                    sem="p4_tm", inc=(1 if half == 1 and j == NDT - 1 else 0))
        if not hasattr(c, "p4_mm"):
            c.p4_mm = {}
        c.p4_mm[cc] = c.val("p4_tm")
        c.wait("scalar", "p4_tm", c.val("p4_tm"))
        if cc >= 1:
            c.wait("scalar", "p4_vm", c.p4_h_done)
        c.emit("scalar", lambda e: e.activation(
            sgf[:, :], g_ps[:, :], AF.Silu, scale=1.0), sem="p4_sm", inc=1)
        c.wait("vector", "p4_sm", c.val("p4_sm"))
        if cc >= 1:
            c.wait("vector", "p4_st", 16 * cc)
        c.emit("vector", lambda e: e.tensor_tensor(
            out=hts[:, :], in0=sgf[:, :], in1=u_ps[:, :], op=ALU.mult),
            sem="p4_vm", inc=1)
        c.p4_h_done = c.val("p4_vm")
        c.wait("sync", "p4_vm", c.p4_h_done)
        c.emit("sync", lambda e, cc=cc: e.dma_start(
            out=io["hT"][PT * cc:PT * cc + PT, :], in_=hts[:, :]),
            sem="p4_st", inc=16)
    c.wait("sync", "p4_st", 16 * 88)

    c.barrier("p4_bar1")
    c.free_sbuf("p4_ht", "p4_sg", "p4_wu1", "p4_wu0", "p4_wg1", "p4_wg0",
                "p4_xn")
    wd = c.sbuf("p4_wd", [PT, 88 * 512], BF16)
    hst = [c.sbuf(f"p4_hs{i}", [PT, 88 * PT], BF16) for i in range(2)]
    exsb = [c.sbuf(f"p4_ex{i}", [PT, 512], F32) for i in range(2)]
    n_ev = 0
    n_st = 0
    last_ev = {0: None, 1: None}
    last_st = {0: None, 1: None}
    for dc in range(8):
        if dc > 0:
            c.wait("sync", "p4_tm", c.p4b_mm)
        c.emit("sync", lambda e, dc=dc: e.dma_start(
            out=wd[:, :].rearrange("p (k q) -> p k q", q=512),
            in_=io["Wd"][:, 512 * dc:512 * dc + 512]
            .rearrange("(k p) q -> p k q", p=PT)), sem="p4_dl", inc=16)
        c.wait("tensor", "p4_dl", c.val("p4_dl"))
        wdv = wd[:, :].rearrange("p (k q) -> p k q", q=512)
        for tt in range(NTT):
            hb = tt % 2
            if not hasattr(c, "p4b_hst_free"):
                c.p4b_hst_free = {0: 0, 1: 0}
            if c.p4b_hst_free[hb]:
                c.wait("sync", "p4_tm", c.p4b_hst_free[hb])
            c.emit("sync", lambda e, dc=dc, tt=tt, hb=hb: e.dma_start(
                out=hst[hb][:, :].rearrange("p (k t) -> p k t", t=PT),
                in_=io["hT"][:, PT * tt:PT * tt + PT]
                .rearrange("(k p) t -> p k t", p=PT)), sem="p4_hl", inc=16)
            c.wait("tensor", "p4_hl", c.val("p4_hl"))
            pb = (dc * 8 + tt) % 2
            if last_ev[pb] is not None:
                c.wait("tensor", "p4_ev", last_ev[pb])
            hv = hst[hb][:, :].rearrange("p (k t) -> p k t", t=PT)
            for kc in range(88):
                c.emit("tensor", lambda e, kc=kc, pb=pb, hv=hv: e.matmul(
                    mmps[pb][:, :], hv[:, kc, :], wdv[:, kc, :],
                    start=(kc == 0), stop=(kc == 87)),
                    sem="p4_tm", inc=(1 if kc == 87 else 0))
            c.p4b_hst_free[hb] = c.val("p4_tm")
            c.p4b_mm = c.val("p4_tm")
            c.wait("vector", "p4_tm", c.val("p4_tm"))
            if last_st[pb] is not None:
                c.wait("vector", "p4_st2", last_st[pb])
            c.emit("vector", lambda e, pb=pb: e.tensor_copy(
                exsb[pb][:, :], mmps[pb][:, :]), sem="p4_ev", inc=1)
            n_ev += 1
            last_ev[pb] = n_ev
            c.wait("sync", "p4_ev", n_ev)
            c.emit("sync", lambda e, tt=tt, dc=dc, pb=pb: e.dma_start(
                out=io["sc"][PT * tt:PT * tt + PT, 512 * dc:512 * dc + 512],
                in_=exsb[pb][:, :]), sem="p4_st2", inc=16)
            n_st += 1
            last_st[pb] = 16 * n_st
    c.wait("sync", "p4_st2", 16 * n_st)
    c.barrier("p4_bar2")
    c.free_sbuf("p4_ex1", "p4_ex0", "p4_hs1", "p4_hs0", "p4_wd")
    c.free_psum("p4_ups", "p4_gps")
    emit_epilogue(c, "p4e_", io["x_old"], io["sc"], io["x_new"], None, ident)


def build_phase4():
    nc = mk_nc()
    io = {
        "xnT": nc.dram_tensor("xnT", [D, TLOC], BF16, kind="ExternalInput").ap(),
        "Wg": nc.dram_tensor("Wg", [D, 11264], BF16, kind="ExternalInput").ap(),
        "Wu": nc.dram_tensor("Wu", [D, 11264], BF16, kind="ExternalInput").ap(),
        "Wd": nc.dram_tensor("Wd", [11264, D], BF16, kind="ExternalInput").ap(),
        "x_old": nc.dram_tensor("x_old", [TLOC, D], F32, kind="ExternalInput").ap(),
        "hT": nc.dram_tensor("hT", [11264, TLOC], BF16).ap(),
        "sc": nc.dram_tensor("sc", [TLOC, D], F32).ap(),
        "x_new": nc.dram_tensor("x_new", [TLOC, D], F32, kind="ExternalOutput").ap(),
    }
    c = Ctx(nc)
    emit_phase4(c, io)
    return c.realize()


def _bf(x):
    return np.ascontiguousarray(x)


def host_prep_common(inputs):
    H = {}
    W = inputs["L0_attn_in_proj_w"]
    H["WoutT"] = np.ascontiguousarray(inputs["L0_attn_out_proj_w"].T)
    A = inputs["L0_ffn_A"]
    H["AT"] = np.ascontiguousarray(A.reshape(1024, 4096).T)
    H["Bp"] = np.ascontiguousarray(
        np.transpose(inputs["L0_ffn_B"], (0, 2, 1)).reshape(1024, 4096))
    H["WrT"] = np.ascontiguousarray(inputs["L0_ffn_Wr"].T)
    H["w_downT"] = np.ascontiguousarray(inputs["L1_attn_w_down"].T)
    H["w_upT"] = np.ascontiguousarray(inputs["L1_attn_w_up"].T)
    H["hy_in_wT"] = np.ascontiguousarray(inputs["L1_attn_hy_in_w"].T)
    H["hy_fcD_wT"] = np.ascontiguousarray(inputs["L1_attn_hy_fcD_w"].T)
    H["hy_out_wT"] = np.ascontiguousarray(inputs["L1_attn_hy_out_w"].T)
    H["cw_xs"] = np.ascontiguousarray(inputs["L1_attn_hy_conv_w"][0:256, 0, 6])
    H["cb_xs"] = np.ascontiguousarray(inputs["L1_attn_hy_conv_b"][0:256])
    return H


def host_prep_core(inputs, c):
    P = {}
    x = inputs["L0_attn_hidden_states"].reshape(8192, 4096)
    P["x0"] = np.ascontiguousarray(x[c * TLOC:(c + 1) * TLOC, :])
    W = inputs["L0_attn_in_proj_w"]
    g0 = 512 * c
    rows = np.concatenate([
        np.arange(g0, g0 + 512),
        np.arange(4096 + g0, 4096 + g0 + 512),
        np.arange(8192 + 128 * c, 8192 + 128 * c + 128),
        np.arange(9216 + 128 * c, 9216 + 128 * c + 128),
        np.arange(10240 + 8 * c, 10240 + 8 * c + 8),
    ])
    P["WinT"] = np.ascontiguousarray(W[rows, :].T)
    ch = np.concatenate([
        np.arange(g0, g0 + 512),
        np.arange(4096 + 128 * c, 4096 + 128 * c + 128),
        np.arange(5120 + 128 * c, 5120 + 128 * c + 128),
    ])
    P["conv_w"] = np.ascontiguousarray(inputs["L0_attn_conv_weight"][ch, :])
    P["conv_b"] = np.ascontiguousarray(inputs["L0_attn_conv_bias"][ch])
    P["dt_bias"] = np.ascontiguousarray(inputs["L0_attn_dt_bias"][8 * c:8 * c + 8])
    P["A_log"] = np.ascontiguousarray(inputs["L0_attn_A_log"][8 * c:8 * c + 8])
    P["Dv"] = np.ascontiguousarray(inputs["L0_attn_D"][8 * c:8 * c + 8])
    P["norm_w"] = np.ascontiguousarray(inputs["L0_attn_norm_weight"][g0:g0 + 512])
    def padcols(a, w):
        out = np.zeros((a.shape[0], w), a.dtype)
        s = a.shape[1]
        out[:, :s] = a
        return out

    def padrows(a, h):
        out = np.zeros((h, a.shape[1]), a.dtype)
        out[:a.shape[0], :] = a
        return out
    i0, i1 = 176 * c, min(176 * (c + 1), 1376)
    P["g_qw"] = padcols(inputs["L1_ffn_gate_qweight"][:, i0:i1], 176)
    P["u_qw"] = padcols(inputs["L1_ffn_up_qweight"][:, i0:i1], 176)
    P["g_qz"] = padcols(inputs["L1_ffn_gate_qzeros"][:, i0:i1], 176)
    P["u_qz"] = padcols(inputs["L1_ffn_up_qzeros"][:, i0:i1], 176)
    j0, j1 = 1408 * c, min(1408 * (c + 1), 11008)
    P["g_sc"] = padcols(inputs["L1_ffn_gate_scales"][:, 8 * i0:8 * i1], 1408)
    P["u_sc"] = padcols(inputs["L1_ffn_up_scales"][:, 8 * i0:8 * i1], 1408)
    P["d_qw"] = padrows(inputs["L1_ffn_down_qweight"][j0:j1, :], 1408)
    s0, s1 = 11 * c, min(11 * (c + 1), 86)
    P["d_sc"] = padrows(inputs["L1_ffn_down_scales"][s0:s1, :], 11)
    P["d_qz"] = padrows(inputs["L1_ffn_down_qzeros"][s0:s1, :], 11)
    return P


def build_phase0():
    nc = mk_nc()
    x0 = nc.dram_tensor("x0", [TLOC, D], F32, kind="ExternalInput").ap()
    xnT = nc.dram_tensor("xnT", [D, TLOC], BF16, kind="ExternalOutput").ap()
    c = Ctx(nc)
    emit_phase0(c, x0, xnT)
    return c.realize()


_BUILD_CACHE = {}


def _get(name, fn):
    if name not in _BUILD_CACHE:
        _BUILD_CACHE[name] = fn()
    return _BUILD_CACHE[name]


def kernel(**inputs):
    H = host_prep_common(inputs)
    P = [host_prep_core(inputs, cc) for cc in range(NC)]

    nc = _get("p0", build_phase0)
    r0 = run_bass_kernel_spmd(nc, [{"x0": P[cc]["x0"]} for cc in range(NC)],
                              core_ids=list(range(NC)))
    xnT_full = np.ascontiguousarray(
        np.concatenate([r0.results[cc]["xnT"] for cc in range(NC)], axis=0))

    nc = _get("p1", build_phase1)
    r1 = run_bass_kernel_spmd(nc, [
        {"xnT_full": xnT_full, "WinT": P[cc]["WinT"], "conv_w": P[cc]["conv_w"],
         "conv_b": P[cc]["conv_b"], "dt_bias": P[cc]["dt_bias"],
         "A_log": P[cc]["A_log"], "Dv": P[cc]["Dv"], "norm_w": P[cc]["norm_w"]}
        for cc in range(NC)], core_ids=list(range(NC)))
    ynT_full = np.concatenate([r1.results[cc]["ynT"] for cc in range(NC)], axis=0)

    nc = _get("p1b", build_phase1b)
    r1b = run_bass_kernel_spmd(nc, [
        {"ynT_own": np.ascontiguousarray(ynT_full[:, TLOC * cc:TLOC * (cc + 1)]),
         "WoutT": H["WoutT"], "x_old": P[cc]["x0"]}
        for cc in range(NC)], core_ids=list(range(NC)))

    nc = _get("p2", build_phase2)
    r2 = run_bass_kernel_spmd(nc, [
        {"xnT": r1b.results[cc]["xnT_next"], "WrT": H["WrT"], "AT": H["AT"],
         "Bp": H["Bp"], "x_old": r1b.results[cc]["x_new"]}
        for cc in range(NC)], core_ids=list(range(NC)))

    nc = _get("p3", build_phase3)
    r3 = run_bass_kernel_spmd(nc, [
        {"xnT": r2.results[cc]["xnT_next"], "w_downT": H["w_downT"],
         "b_down": inputs["L1_attn_b_down"], "w_upT": H["w_upT"],
         "b_up": inputs["L1_attn_b_up"],
         "gate1": np.asarray(inputs["L1_attn_gate1"]).reshape(1),
         "hy_in_wT": H["hy_in_wT"], "cw_xs": H["cw_xs"], "cb_xs": H["cb_xs"],
         "hy_D": inputs["L1_attn_hy_D"], "hy_fcD_wT": H["hy_fcD_wT"],
         "hy_norm_w": inputs["L1_attn_hy_norm_w"], "hy_out_wT": H["hy_out_wT"],
         "x_old": r2.results[cc]["x_new"]}
        for cc in range(NC)], core_ids=list(range(NC)))

    nc = _get("pd", build_phaseD)
    rd = run_bass_kernel_spmd(nc, [
        {k: P[cc][k] for k in ("g_qw", "g_sc", "g_qz", "u_qw", "u_sc", "u_qz",
                               "d_qw", "d_sc", "d_qz")}
        for cc in range(NC)], core_ids=list(range(NC)))
    Wg = np.ascontiguousarray(np.concatenate(
        [rd.results[cc]["Wg_c"] for cc in range(NC)], axis=1))
    Wu = np.ascontiguousarray(np.concatenate(
        [rd.results[cc]["Wu_c"] for cc in range(NC)], axis=1))
    Wd = np.ascontiguousarray(np.concatenate(
        [rd.results[cc]["Wd_c"] for cc in range(NC)], axis=0))

    nc = _get("p4", build_phase4)
    r4 = run_bass_kernel_spmd(nc, [
        {"xnT": r3.results[cc]["xnT_next"], "Wg": Wg, "Wu": Wu, "Wd": Wd,
         "x_old": r3.results[cc]["x_new"]}
        for cc in range(NC)], core_ids=list(range(NC)))

    out = np.concatenate([r4.results[cc]["x_new"] for cc in range(NC)], axis=0)
    return out.reshape(1, 8192, 4096).astype(np.float32)
